# version 13 (final)
# speedup vs baseline: 1.1756x; 1.0019x over previous
import numpy as np
from contextlib import ExitStack
import concourse.bass as bass
import concourse.mybir as mybir
from concourse.bass_utils import run_bass_kernel_spmd
from concourse.aot_env import allow_ici_all_to_all

F32 = mybir.dt.float32
AF = mybir.ActivationFunctionType
ALU = mybir.AluOpType
AX = mybir.AxisListType

NCORES = 8
B = 2
T = 8192
D = 2048
TPC = T // NCORES
IPD = 5152
NVH, NKH, DK, DV = 16, 8, 64, 128
SCALE = 0.125
SH_CH = 776
C = 128
NCH = T // C
EPS_LN = 1e-5
EPS_RMS = 1e-6
INTER = 8192


class Sem:
    def __init__(self, nc, ex, name):
        cm = nc.semaphore(name)
        self.h = cm.__enter__()
        ex.callback(cm.__exit__, None, None, None)
        self.n = 0

    def inc(self, instr, k=1):
        instr.then_inc(self.h, k)
        self.n += k
        return self.n


class Chain:
    def __init__(self, nc, ex, name):
        self.s = Sem(nc, ex, name)

    def mark(self, instr):
        self.s.inc(instr)
        return instr

    def join(self, eng):
        if self.s.n:
            eng.wait_ge(self.s.h, self.s.n)


def sbuf(nc, ex, name, shape):
    return ex.enter_context(nc.sbuf_tensor(name, shape, F32))


def psum(nc, ex, name, shape):
    return ex.enter_context(nc.psum_tensor(name, shape, F32))


def host_constants():
    c = {}
    c["ident"] = np.eye(128, dtype=np.float32)
    c["ones"] = np.ones((128, 128), dtype=np.float32)
    H = np.array([[1.0]], dtype=np.float32)
    while H.shape[0] < 128:
        H = np.block([[H, H], [H, -H]]).astype(np.float32)
    c["h128"] = np.ascontiguousarray(H)
    idx = np.arange(C)
    c["pen_nat"] = np.where(idx[:, None] < idx[None, :], 1e30, 0.0).astype(np.float32)
    c["pen_t"] = np.where(idx[:, None] > idx[None, :], -1e30, 0.0).astype(np.float32)
    c["mask_lsn"] = np.where(idx[:, None] > idx[None, :], -1.0, 0.0).astype(np.float32)
    c["mask_usn"] = np.where(idx[:, None] < idx[None, :], -1.0, 0.0).astype(np.float32)
    mis = np.zeros((128, 4), dtype=np.float32)
    mis[:, 0] = EPS_LN
    mis[:, 1] = EPS_RMS
    mis[:, 2] = 1.0
    c["miscc"] = mis
    return c


CONST_SHAPES = [("ident", (128, 128)), ("ones", (128, 128)), ("h128", (128, 128)),
                ("pen_nat", (128, 128)), ("pen_t", (128, 128)),
                ("mask_lsn", (128, 128)), ("mask_usn", (128, 128)),
                ("miscc", (128, 4))]


def load_consts(nc, block, ex, names):
    aps, tiles = {}, {}
    for n, shp in CONST_SHAPES:
        if n not in names:
            continue
        aps[n] = nc.dram_tensor("c_" + n, list(shp), F32, kind="ExternalInput").ap()
        tiles[n] = sbuf(nc, ex, "ct_" + n, list(shp))
    sC = Sem(nc, ex, "s_const")

    @block.sync
    def _(sync):
        for n in tiles:
            sC.inc(sync.dma_start(out=tiles[n][:, :], in_=aps[n][:, :]), 16)
    tiles["_sem"] = sC
    return tiles


STRIPS = [(s * 512, 512) for s in range(10)] + [(5120, 32)]


def p1_mm_uses():
    uses = []
    for b in range(B):
        for si, (c0, w) in enumerate(STRIPS):
            for ci in range(max(w // 128, 1)):
                for tc in range(2):
                    uses.append((b, si, ci, tc, (c0 + ci * 128) // 128, w))
    return uses


def emit_phase1(nc, block, ex, g, cc1in, z_scr, xn_scr, x_in, wT):
    NXT = TPC // 128
    xslot = [sbuf(nc, ex, f"p1x{i}", [128, 2048]) for i in range(2)]
    xnt = [sbuf(nc, ex, f"p1n{i}", [128, 2048]) for i in range(2)]
    xnfm = [sbuf(nc, ex, f"p1f{k}", [128, TPC]) for k in range(16)]
    wslot = [sbuf(nc, ex, f"p1w{i}", [128, 16 * 512]) for i in range(2)]
    stage = [sbuf(nc, ex, f"p1s{i}", [128, 512]) for i in range(4)]
    sqt = sbuf(nc, ex, "p1sq", [128, 2048])
    mcol = [sbuf(nc, ex, f"p1m{i}", [128, 1]) for i in range(2)]
    vcol = [sbuf(nc, ex, f"p1v{i}", [128, 1]) for i in range(2)]
    scol = [sbuf(nc, ex, f"p1c{i}", [128, 1]) for i in range(2)]
    rcol = [sbuf(nc, ex, f"p1r{i}", [128, 1]) for i in range(2)]
    ptr = [psum(nc, ex, f"p1pt{i}", [128, 512]) for i in range(2)]
    pmm = [psum(nc, ex, f"p1pm{i}", [128, 512]) for i in range(2)]

    sX = [Sem(nc, ex, "p1sX0"), Sem(nc, ex, "p1sX1")]
    sVar = Sem(nc, ex, "p1sVa")
    sSq = Sem(nc, ex, "p1sSq")
    sLN = Sem(nc, ex, "p1sLN")
    sTR = Sem(nc, ex, "p1sTR")
    sXE = Sem(nc, ex, "p1sXE")
    sW = [Sem(nc, ex, "p1sW0"), Sem(nc, ex, "p1sW1")]
    sMM = Sem(nc, ex, "p1sMM")
    sEV = Sem(nc, ex, "p1sEV")
    sST = [Sem(nc, ex, f"p1sST{i}") for i in range(4)]
    sXN = Sem(nc, ex, "p1sXN")
    vch = Chain(nc, ex, "p1vch")

    mm_uses = p1_mm_uses()
    PERB = len(mm_uses) // B
    strip_last_ui = {}
    for ui, (b, si, ci, tc, gi, w) in enumerate(mm_uses):
        strip_last_ui[b * len(STRIPS) + si] = ui

    def dests(b, tc, gi):
        out = []
        cs = slice(tc * 512, (tc + 1) * 512)
        if gi < 4:
            out.append((cc1in[4 * b + gi, 0:128, cs], (0, 128)))
        elif gi < 8:
            out.append((cc1in[4 * b + gi - 4, 128:256, cs], (0, 128)))
        elif gi < 24:
            u, s4 = (gi - 8) // 4, (gi - 8) % 4
            out.append((cc1in[4 * b + u, 256 + 128 * s4:384 + 128 * s4, cs], (0, 128)))
        elif gi < 40:
            out.append((z_scr[b, gi - 24, :, cs], (0, 128)))
        else:
            for u in range(4):
                out.append((cc1in[4 * b + u, 768:772, cs], (4 * u, 4 * u + 4)))
                out.append((cc1in[4 * b + u, 772:776, cs], (16 + 4 * u, 16 + 4 * u + 4)))
        return out

    st_cum = []
    slot_tot = [0, 0, 0, 0]
    for ui, (b, si, ci, tc, gi, w) in enumerate(mm_uses):
        sl = ui % 4
        slot_tot[sl] += 16 * len(dests(b, tc, gi))
        st_cum.append((sl, slot_tot[sl]))

    @block.sync
    def _(sync):
        for b in range(B):
            for t8 in range(NXT):
                i = b * NXT + t8
                if i >= 2:
                    sync.wait_ge(sLN.h, i - 1)
                sX[i % 2].inc(sync.dma_start(out=xslot[i % 2][:, :],
                                             in_=x_in[b, t8 * 128:(t8 + 1) * 128, :]), 16)
            for si, (c0, w) in enumerate(STRIPS):
                gsi = b * len(STRIPS) + si
                if gsi >= 2:
                    sync.wait_ge(sEV.h, strip_last_ui[gsi - 2] + 1)
                for k in range(16):
                    sW[gsi % 2].inc(sync.dma_start(
                        out=wslot[gsi % 2][:, k * 512:k * 512 + w],
                        in_=wT[k * 128:(k + 1) * 128, c0:c0 + w]), 16)

    @block.vector
    def _(v):
        def tevacs(b, t8):
            for gq in range(4):
                gidx = (b * NXT + t8) * 4 + gq
                v.wait_ge(sTR.h, gidx + 1)
                pt = ptr[gidx % 2]
                ev = None
                for q4 in range(4):
                    ct = gq * 4 + q4
                    ev = v.tensor_copy(xnfm[ct][:, t8 * 128:(t8 + 1) * 128],
                                       pt[:, q4 * 128:(q4 + 1) * 128])
                sXE.inc(ev)

        for b in range(B):
            if b == 1:
                v.wait_ge(sXN.h, 256)
            for t8 in range(NXT):
                i = b * NXT + t8
                s = i % 2
                v.wait_ge(sX[i % 2].h, 16 * (i // 2 + 1))
                if i >= 2:
                    v.wait_ge(sTR.h, 4 * (i - 1))
                xs, xn = xslot[s], xnt[s]
                vch.join(v)
                vch.mark(v.tensor_reduce(mcol[s][:, :], xs[:, :], axis=AX.X, op=ALU.add))
                vch.mark(v.tensor_tensor(out=sqt[:, :], in0=xs[:, :], in1=xs[:, :], op=ALU.mult))
                vch.join(v)
                vch.mark(v.tensor_reduce(vcol[s][:, :], sqt[:, :], axis=AX.X, op=ALU.add))
                vch.mark(v.tensor_scalar(out=mcol[s][:, :], in0=mcol[s][:, :],
                                         scalar1=1.0 / D, scalar2=None, op0=ALU.mult))
                vch.join(v)
                vch.mark(v.tensor_tensor(out=rcol[s][:, :], in0=mcol[s][:, :],
                                         in1=mcol[s][:, :], op=ALU.mult))
                vch.join(v)
                sVar.inc(v.scalar_tensor_tensor(out=vcol[s][:, :], in0=vcol[s][:, :],
                                                scalar=1.0 / D, in1=rcol[s][:, :],
                                                op0=ALU.mult, op1=ALU.subtract))
                v.wait_ge(sSq.h, i + 1)
                vch.mark(v.reciprocal(out=rcol[s][:, :], in_=scol[s][:, :]))
                vch.join(v)
                sLN.inc(v.tensor_scalar(out=xn[:, :], in0=xs[:, :],
                                        scalar1=mcol[s][:, :], scalar2=rcol[s][:, :],
                                        op0=ALU.subtract, op1=ALU.mult))
                if t8 >= 1:
                    tevacs(b, t8 - 1)
            tevacs(b, NXT - 1)
            for ui in range(b * PERB, (b + 1) * PERB):
                (bb, si, ci, tc, gi, w) = mm_uses[ui]
                v.wait_ge(sMM.h, ui + 1)
                if ui >= 4:
                    sl, cum = st_cum[ui - 4]
                    v.wait_ge(sST[sl].h, cum)
                rows = 128 if w >= 128 else 32
                sEV.inc(v.tensor_copy(stage[ui % 4][0:rows, :], pmm[ui % 2][0:rows, :]))

    @block.scalar
    def _(sc):
        for i in range(B * NXT):
            s = i % 2
            sc.wait_ge(sVar.h, i + 1)
            sc.wait_ge(g["_sem"].h, g["_sem"].n)
            sSq.inc(sc.activation(out=scol[s][:, :], in_=vcol[s][:, :],
                                  func=AF.Sqrt, bias=g["miscc"][:, 0:1], scale=1.0))

    @block.tensor
    def _(t):
        t.wait_ge(g["_sem"].h, g["_sem"].n)
        for b in range(B):
            for t8 in range(NXT):
                i = b * NXT + t8
                t.wait_ge(sLN.h, i + 1)
                for gq in range(4):
                    gidx = i * 4 + gq
                    if gidx >= 2:
                        t.wait_ge(sXE.h, gidx - 1)
                    pt = ptr[gidx % 2]
                    mm = None
                    for q4 in range(4):
                        ct = gq * 4 + q4
                        mm = t.transpose(pt[:, q4 * 128:(q4 + 1) * 128],
                                         xnt[i % 2][:, ct * 128:(ct + 1) * 128],
                                         g["ident"][:, :])
                    sTR.inc(mm)
            t.wait_ge(sXE.h, (b + 1) * NXT * 4)
            for ui in range(b * PERB, (b + 1) * PERB):
                (bb, si, ci, tc, gi, w) = mm_uses[ui]
                gsi = b * len(STRIPS) + si
                if ci == 0 and tc == 0:
                    t.wait_ge(sW[gsi % 2].h, 256 * (gsi // 2 + 1))
                if ui >= 2:
                    t.wait_ge(sEV.h, ui - 1)
                ws = wslot[gsi % 2]
                pm = pmm[ui % 2]
                rows = 128 if w >= 128 else 32
                mm = None
                for k in range(16):
                    mm = t.matmul(pm[0:rows, :],
                                  ws[:, k * 512 + ci * 128:k * 512 + ci * 128 + rows],
                                  xnfm[k][:, tc * 512:(tc + 1) * 512],
                                  start=(k == 0), stop=(k == 15))
                sMM.inc(mm)

    @block.gpsimd
    def _(gp):
        for b in range(B):
            for ui in range(b * PERB, (b + 1) * PERB):
                (bb, si, ci, tc, gi, w) = mm_uses[ui]
                gp.wait_ge(sEV.h, ui + 1)
                for ap, (r0, r1) in dests(b, tc, gi):
                    sST[ui % 4].inc(gp.dma_start(out=ap, in_=stage[ui % 4][r0:r1, :]), 16)
            for ct in range(16):
                sXN.inc(gp.dma_start(out=xn_scr[b, ct, :, :], in_=xnfm[ct][:, :]), 16)
        for s4 in range(4):
            gp.wait_ge(sST[s4].h, sST[s4].n)
        gp.wait_ge(sXN.h, sXN.n)

    return dict(sST=sST, sXN=sXN)




P3_MAXSTAGE = 99
P3_SUB = 99

class ES:
    def __init__(self, nc, ex, name):
        self.sem = Sem(nc, ex, name)

    def do(self, instr):
        self.sem.inc(instr, 1)
        return self.sem.n

    @property
    def tok(self):
        return self.sem.n

    @property
    def h(self):
        return self.sem.h


def emit_phase3(nc, block, ex, g, cc1out, cc2in, alog4, dtb4, cwt, ngroups=None,
                entry=None, wdprep=None):
    NG = ngroups if ngroups is not None else T // 512
    convin = [[sbuf(nc, ex, f"p3ci{s}_{t}", [128, 515]) for t in range(6)] for s in range(2)]
    accb = [[sbuf(nc, ex, f"p3ac{s}_{t}", [128, 512]) for t in range(6)] for s in range(2)]
    qkvb = [[sbuf(nc, ex, f"p3qk{s}_{t}", [128, 512]) for t in range(6)] for s in range(2)]
    abb = [sbuf(nc, ex, f"p3ab{s}", [4, 512]) for s in range(2)]
    aba = [sbuf(nc, ex, f"p3aa{s}", [4, 512]) for s in range(2)]
    bet4 = [sbuf(nc, ex, f"p3be{s}", [4, 512]) for s in range(2)]
    g4 = [sbuf(nc, ex, f"p3g4{s}", [4, 512]) for s in range(2)]
    spx = sbuf(nc, ex, "p3spx", [4, 512])
    spe = sbuf(nc, ex, "p3spe", [4, 512])
    ytmp = sbuf(nc, ex, "p3ytm", [4, 512])
    gtmp = sbuf(nc, ex, "p3gt", [4, 128])
    pkB = sbuf(nc, ex, "p3pkB", [4, 128])
    pkNB = sbuf(nc, ex, "p3pkN", [4, 128])
    pkBE = sbuf(nc, ex, "p3pkE", [4, 128])
    brh = [sbuf(nc, ex, f"p3br{h}", [1, 128]) for h in range(4)]
    bcrow = sbuf(nc, ex, "p3bcr", [1, 4])
    packTs = sbuf(nc, ex, "p3pts", [128, 12])
    eB = sbuf(nc, ex, "p3eb", [128, 4])
    wc4 = sbuf(nc, ex, "p3wc", [128, 4])
    brv = sbuf(nc, ex, "p3brv", [128, 4])
    kq4 = sbuf(nc, ex, "p3kq", [128, 4])
    sq4 = sbuf(nc, ex, "p3sq4", [128, 4])
    rc4 = sbuf(nc, ex, "p3rc", [128, 4])
    rq4 = sbuf(nc, ex, "p3rq", [128, 4])
    rr4 = sbuf(nc, ex, "p3rr", [128, 4])
    rmsc = sbuf(nc, ex, "p3rm", [128, 4])
    ksq = sbuf(nc, ex, "p3ksq", [128, 128])
    qsq = sbuf(nc, ex, "p3qsq", [128, 128])
    tkpair = sbuf(nc, ex, "p3tkp", [128, 128])
    tkbA = sbuf(nc, ex, "p3tba", [128, 128])
    tkbB = sbuf(nc, ex, "p3tbb", [128, 128])
    dnat = [sbuf(nc, ex, f"p3dn{h}", [128, 128]) for h in range(4)]
    dtt = [sbuf(nc, ex, f"p3dt{h}", [128, 128]) for h in range(4)]
    anat = [sbuf(nc, ex, f"p3an{h}", [128, 128]) for h in range(4)]
    att_ = [sbuf(nc, ex, f"p3at{h}", [128, 128]) for h in range(4)]
    nnb = [[sbuf(nc, ex, f"p3nn{h}_{i}", [128, 128]) for i in range(2)] for h in range(4)]
    ntb = [[sbuf(nc, ex, f"p3nt{h}_{i}", [128, 128]) for i in range(2)] for h in range(4)]
    delt = [sbuf(nc, ex, f"p3de{h}", [128, 128]) for h in range(4)]
    obuf = [sbuf(nc, ex, f"p3ob{h}", [128, 128]) for h in range(4)]
    osq = sbuf(nc, ex, "p3osq", [128, 128])
    oship = [sbuf(nc, ex, f"p3os{h}", [128, 128]) for h in range(8)]
    SAB = [sbuf(nc, ex, f"p3S{i}", [128, 128]) for i in range(2)]
    khs = [sbuf(nc, ex, f"p3khs{p}", [128, 128]) for p in range(2)]
    khb02 = [sbuf(nc, ex, f"p3kb0{p}", [128, 128]) for p in range(2)]
    khb13 = [sbuf(nc, ex, f"p3kb1{p}", [128, 128]) for p in range(2)]
    tkbar = [sbuf(nc, ex, f"p3tbr{p}", [128, 256]) for p in range(2)]
    attq = [[sbuf(nc, ex, f"p3aq{p}_{h}", [128, 128]) for h in range(4)] for p in range(2)]
    pbuf = [[sbuf(nc, ex, f"p3pb{p}_{h}", [128, 128]) for h in range(4)] for p in range(2)]
    rhsb = [[sbuf(nc, ex, f"p3rh{p}_{h}", [128, 128]) for h in range(4)] for p in range(2)]
    tvs = [[sbuf(nc, ex, f"p3tv{p}_{h}", [128, 128]) for h in range(4)] for p in range(2)]
    nbeb = [sbuf(nc, ex, f"p3nb{p}", [128, 4]) for p in range(2)]
    eBCb = [sbuf(nc, ex, f"p3ebc{p}", [128, 4]) for p in range(2)]
    s2c = [sbuf(nc, ex, f"p3s2{p}", [128, 4]) for p in range(2)]
    s1c = [sbuf(nc, ex, f"p3s1{p}", [128, 4]) for p in range(2)]
    eA4 = sbuf(nc, ex, "p3ea", [4, 1])
    dtb_s = sbuf(nc, ex, "p3db", [4, 1])
    alog_s = sbuf(nc, ex, "p3al", [4, 1])
    cw_s = sbuf(nc, ex, "p3cw", [128, 24])
    bankG = psum(nc, ex, "p3bG", [128, 512])
    bankA = psum(nc, ex, "p3bA", [128, 512])
    bankB = psum(nc, ex, "p3bB", [128, 512])
    bankD = psum(nc, ex, "p3bD", [128, 512])
    bankE = psum(nc, ex, "p3bE", [128, 512])
    bankH = psum(nc, ex, "p3bH", [128, 512])
    bankF = psum(nc, ex, "p3bF", [128, 512])
    bankM = psum(nc, ex, "p3bM", [128, 512])
    ptk = bankG[:, 0:128]
    pkh = bankG[:, 128:256]
    pkb = [bankG[:, 256:384], bankG[:, 384:512]]
    ptv = bankA
    ppr = bankA
    pbb = bankB
    pg3 = bankB
    pg1 = bankD
    pdd = [bankE, bankD]
    pg2 = bankE
    ppk = bankH[:, 256:272]
    psm = bankH[:, 272:276]
    pbc = bankH[:, 276:280]
    po1 = bankF
    pks = bankM
    pdl = bankM
    po2 = bankM
    puu = [bankM[:, 0:128], bankM[:, 128:256]]
    pot = bankM

    SV = ES(nc, ex, "p3SV")
    SS = ES(nc, ex, "p3SS")
    STN = ES(nc, ex, "p3ST")
    sCV = [Sem(nc, ex, f"p3cv{i}") for i in range(2)]
    sO = [Sem(nc, ex, f"p3so{h}") for h in range(8)]
    sPRE = Sem(nc, ex, "p3pre")
    sDB1 = Sem(nc, ex, "p3db1")
    sDB2 = Sem(nc, ex, "p3db2")

    IDn = g["ident"]
    ONE = g["ones"]
    MISC = g["miscc"]

    @block.sync
    def _(sync):
        if entry is not None:
            sync.wait_ge(entry[0], entry[1])
        for ap_h, ap_s in [(alog4, alog_s), (dtb4, dtb_s), (cwt, cw_s)]:
            sPRE.inc(sync.dma_start(out=ap_s[:, :], in_=ap_h[:, :]), 16)

    @block.scalar
    def _(sc):
        sc.wait_ge(SS.h, SS.tok)
        sc.wait_ge(sPRE.h, sPRE.n)
        sc.wait_ge(g["_sem"].h, g["_sem"].n)
        SS.do(sc.activation(out=eA4[:, :], in_=alog_s[:, :], func=AF.Exp))

    @block.vector
    def _(v):
        v.wait_ge(SV.h, SV.tok)
        v.wait_ge(g["_sem"].h, g["_sem"].n)
        if entry is not None:
            v.wait_ge(entry[0], entry[1])
        for t in range(6):
            SV.do(v.memset(convin[0][t][:, 0:3], 0.0))
        for i in range(2):
            SV.do(v.memset(SAB[i][:, :], 0.0))

    tok_conv = {}
    tok_silu = {}
    tok_silu_all = {}
    tok_ab = {}
    tok_qdone = {-1: 0}
    C = {}

    def group_sections(G):
        sl = G % 2

        @block.sync
        def _(sync, G=G, sl=sl):
            shard = G // 2
            c0 = (G % 2) * 512
            if G >= 2:
                sync.wait_ge(SV.h, tok_conv[G - 1])
                sync.wait_ge(SV.h, tok_ab[G - 2])
                sync.wait_ge(STN.h, tok_qdone[4 * (G - 2) + 3])
            for t in range(6):
                sCV[sl].inc(sync.dma_start(
                    out=convin[sl][t][:, 3:515],
                    in_=cc1out[shard, 128 * t:128 * (t + 1), c0:c0 + 512]), 16)
            sCV[sl].inc(sync.dma_start(
                out=abb[sl][:, :], in_=cc1out[shard, 768:772, c0:c0 + 512]), 16)
            sCV[sl].inc(sync.dma_start(
                out=aba[sl][:, :], in_=cc1out[shard, 772:776, c0:c0 + 512]), 16)

        @block.vector
        def _(v, G=G, sl=sl):
            v.wait_ge(SV.h, SV.tok)
            v.wait_ge(sCV[sl].h, sCV[sl].n)
            if G >= 1:
                v.wait_ge(sCV[(G - 1) % 2].h, sCV[(G - 1) % 2].n)
                for t in range(6):
                    SV.do(v.tensor_copy(convin[sl][t][:, 0:3],
                                        convin[(G - 1) % 2][t][:, 512:515]))
            tcp = SV.tok
            for t in range(6):
                if G >= 2:
                    v.wait_ge(SS.h, tok_silu[(G - 2, t)])
                if G >= 1:
                    v.wait_ge(SV.h, tcp)
                acc = accb[sl][t]
                ci = convin[sl][t]
                wof = 4 * t
                v.wait_ge(sPRE.h, sPRE.n)
                SV.do(v.tensor_scalar(out=acc[:, :], in0=ci[:, 0:512],
                                      scalar1=cw_s[:, wof:wof + 1], scalar2=None,
                                      op0=ALU.mult))
                for k in range(1, 4):
                    tk = SV.tok
                    v.wait_ge(SV.h, tk)
                    SV.do(v.scalar_tensor_tensor(out=acc[:, :], in0=ci[:, k:k + 512],
                                                 scalar=cw_s[:, wof + k:wof + k + 1],
                                                 in1=acc[:, :], op0=ALU.mult, op1=ALU.add))
            tok_conv[G] = SV.tok

        @block.scalar
        def _(sc, G=G, sl=sl):
            sc.wait_ge(SS.h, SS.tok)
            sc.wait_ge(SV.h, tok_conv[G])
            if G >= 2:
                sc.wait_ge(STN.h, tok_qdone[4 * (G - 2) + 3])
                sc.wait_ge(SS.h, C[4 * (G - 2) + 3]["S14"])
            for t in range(6):
                tok_silu[(G, t)] = SS.do(sc.activation(out=qkvb[sl][t][:, :],
                                                       in_=accb[sl][t][:, :],
                                                       func=AF.Silu))
            tok_silu_all[G] = SS.tok
            sc.wait_ge(sPRE.h, sPRE.n)
            SS.do(sc.activation(out=bet4[sl][:, :], in_=abb[sl][:, :], func=AF.Sigmoid))

        @block.vector
        def _(v, G=G, sl=sl):
            v.wait_ge(SV.h, SV.tok)
            v.wait_ge(sCV[sl].h, sCV[sl].n)
            v.wait_ge(sPRE.h, sPRE.n)
            if G >= 2:
                v.wait_ge(SV.h, tok_ab[G - 2])
                v.wait_ge(SV.h, C[4 * (G - 2) + 3]["V1"])
            SV.do(v.tensor_scalar(out=ytmp[:, :], in0=aba[sl][:, :],
                                  scalar1=dtb_s[:, :], scalar2=None, op0=ALU.add))
            SV.do(v.tensor_scalar(out=g4[sl][:, :], in0=aba[sl][:, :],
                                  scalar1=dtb_s[:, :], scalar2=0.0,
                                  op0=ALU.add, op1=ALU.max))
        tok_spv = SV.tok

        @block.scalar
        def _(sc, G=G, tok_spv=tok_spv):
            sc.wait_ge(SS.h, SS.tok)
            sc.wait_ge(SV.h, tok_spv)
            SS.do(sc.activation(out=spx[:, :], in_=ytmp[:, :], func=AF.Abs))
            t0 = SS.tok
            sc.wait_ge(SS.h, t0)
            SS.do(sc.activation(out=spe[:, :], in_=spx[:, :], func=AF.Exp, scale=-1.0))
            t1 = SS.tok
            sc.wait_ge(SS.h, t1)
            SS.do(sc.activation(out=spe[:, :], in_=spe[:, :], func=AF.Ln,
                                bias=MISC[0:4, 2:3], scale=1.0))
        tok_spe = SS.tok

        @block.vector
        def _(v, G=G, sl=sl, tok_spe=tok_spe):
            v.wait_ge(SV.h, SV.tok)
            v.wait_ge(SS.h, tok_spe)
            SV.do(v.tensor_tensor(out=g4[sl][:, :], in0=g4[sl][:, :], in1=spe[:, :],
                                  op=ALU.add))
        tok_ab[G] = SV.tok

    def prep_chunk(ch):
        G, c4 = ch // 4, ch % 4
        sl = G % 2
        pb = ch % 2
        csl = slice(128 * c4, 128 * (c4 + 1))
        c = C[ch] = {}
        cp = C.get(ch - 1, {})

        @block.vector
        def _(v):
            v.wait_ge(SV.h, SV.tok)
            v.wait_ge(SV.h, tok_ab[G])
            v.wait_ge(STN.h, C[ch - 1]["T1"] if ch >= 1 else 0)
            v.wait_ge(sDB1.h, sDB1.n)
            SV.do(v.tensor_scalar(out=gtmp[:, :], in0=g4[sl][:, csl],
                                  scalar1=eA4[:, :], scalar2=-1.0,
                                  op0=ALU.mult, op1=ALU.mult))
            t1 = SV.tok
            v.wait_ge(SV.h, t1)
            SV.do(v.tensor_tensor_scan(out=pkB[:, :], data0=gtmp[:, :],
                                       data1=gtmp[:, :], initial=0.0,
                                       op0=ALU.add, op1=ALU.bypass))
            t2 = SV.tok
            v.wait_ge(SV.h, t2)
            SV.do(v.tensor_scalar(out=pkNB[:, :], in0=pkB[:, :],
                                  scalar1=-1.0, scalar2=None, op0=ALU.mult))
            SV.do(v.tensor_copy(pkBE[:, :], bet4[sl][:, csl]))
        c["V1"] = SV.tok

        @block.sync
        def _(sync):
            sync.wait_ge(SV.h, c["V1"])
            sync.wait_ge(STN.h, C[ch - 1]["T1"] if ch >= 1 else 0)
            for h in range(4):
                sDB1.inc(sync.dma_start(out=brh[h][0:1, :], in_=pkB[h:h + 1, :]), 16)

        @block.tensor
        def _(t):
            if ch == 0:
                t.wait_ge(g["_sem"].h, g["_sem"].n)
            t.wait_ge(SV.h, c["V1"])
            if ch >= 1:
                t.wait_ge(SS.h, C[ch - 1]["S5b"])
            t.wait_ge(SS.h, tok_silu_all[G])
            t.wait_ge(sDB1.h, sDB1.n)
            STN.do(t.transpose(ppk[:, 0:4], pkB[:, :], IDn[0:4, 0:4]))
            STN.do(t.transpose(ppk[:, 4:8], pkNB[:, :], IDn[0:4, 0:4]))
            STN.do(t.transpose(ppk[:, 8:12], pkBE[:, :], IDn[0:4, 0:4]))
            for h in range(4):
                STN.do(t.transpose(ptv[:, 128 * h:128 * (h + 1)],
                                   qkvb[sl][2 + h][:, csl], IDn[:, :]))
            STN.do(t.transpose(ptk[:, 0:128], qkvb[sl][1][:, csl], IDn[:, :]))
            for h in range(4):
                STN.do(t.matmul(pbb[:, 128 * h:128 * (h + 1)],
                                ONE[0:1, 0:128], brh[h][0:1, :],
                                start=True, stop=True))
        c["T1"] = STN.tok

        @block.scalar
        def _(sc):
            sc.wait_ge(SS.h, SS.tok)
            sc.wait_ge(STN.h, c["T1"])
            if ch >= 1:
                sc.wait_ge(SV.h, C[ch - 1]["V4b"])
            sc.wait_ge(sDB2.h, sDB2.n)
            SS.do(sc.activation(out=ksq[:, :], in_=qkvb[sl][1][:, csl], func=AF.Square))
            SS.do(sc.activation(out=qsq[:, :], in_=qkvb[sl][0][:, csl], func=AF.Square))
            SS.do(sc.copy(out=packTs[:, 0:12], in_=ppk[:, 0:12]))
            t1 = SS.tok
            sc.wait_ge(SS.h, t1)
            SS.do(sc.activation(out=eB[:, :], in_=packTs[:, 0:4], func=AF.Exp))
        c["S2"] = SS.tok

        @block.sync
        def _(sync):
            sync.wait_ge(SS.h, c["S2"])
            sync.wait_ge(STN.h, c["T1"])
            sDB2.inc(sync.dma_start(out=bcrow[:, :], in_=packTs[127:128, 0:4]), 16)

        @block.tensor
        def _(t):
            t.wait_ge(SS.h, c["S2"])
            t.wait_ge(sDB2.h, sDB2.n)
            if ch >= 1:
                t.wait_ge(SV.h, C[ch - 1]["V4a"])
            for kh in range(2):
                r0 = 64 * kh
                STN.do(t.matmul(psm[:, kh:kh + 1], ksq[r0:r0 + 64, :],
                                ONE[r0:r0 + 64, 0:1], start=True, stop=True))
                STN.do(t.matmul(psm[:, 2 + kh:3 + kh], qsq[r0:r0 + 64, :],
                                ONE[r0:r0 + 64, 0:1], start=True, stop=True))
            STN.do(t.matmul(pbc[:, 0:4], ONE[0:1, 0:128], bcrow[0:1, 0:4],
                            start=True, stop=True))
        c["T3"] = STN.tok

        @block.vector
        def _(v):
            v.wait_ge(SV.h, SV.tok)
            v.wait_ge(STN.h, c["T3"])
            SV.do(v.tensor_scalar(out=kq4[:, :], in0=psm[:, 0:4], scalar1=1e-24,
                                  scalar2=None, op0=ALU.max))
            SV.do(v.tensor_tensor(out=brv[:, :], in0=pbc[:, 0:4], in1=packTs[:, 0:4],
                                  op=ALU.subtract))
            for h in range(4):
                SV.do(v.tensor_tensor(out=dnat[h][:, :], in0=pbb[:, 128 * h:128 * (h + 1)],
                                      in1=g["pen_nat"][:, :], op=ALU.add))
                SV.do(v.tensor_tensor(out=dtt[h][:, :], in0=pbb[:, 128 * h:128 * (h + 1)],
                                      in1=g["pen_t"][:, :], op=ALU.add))
                SV.do(v.tensor_copy(tvs[pb][h][:, :], ptv[:, 128 * h:128 * (h + 1)]))
        c["V4a"] = SV.tok

        @block.scalar
        def _(sc):
            sc.wait_ge(SS.h, SS.tok)
            sc.wait_ge(SV.h, c["V4a"])
            SS.do(sc.activation(out=sq4[:, :], in_=kq4[:, :], func=AF.Sqrt))
            SS.do(sc.activation(out=wc4[:, :], in_=brv[:, :], func=AF.Exp))
            SS.do(sc.activation(out=eBCb[pb][:, :], in_=pbc[:, 0:4], func=AF.Exp))
            for h in range(4):
                SS.do(sc.activation(out=dnat[h][:, :], in_=dnat[h][:, :], func=AF.Exp,
                                    bias=packTs[:, h:h + 1], scale=-1.0))
                SS.do(sc.activation(out=dtt[h][:, :], in_=dtt[h][:, :], func=AF.Exp,
                                    bias=packTs[:, 4 + h:5 + h], scale=1.0))
        c["S4"] = SS.tok

        @block.vector
        def _(v):
            v.wait_ge(SV.h, SV.tok)
            v.wait_ge(SS.h, c["S4"])
            SV.do(v.reciprocal(out=rc4[:, :], in_=sq4[:, :]))
            tr = SV.tok
            v.wait_ge(SV.h, tr)
            for kh in range(2):
                SV.do(v.tensor_scalar(out=tkpair[:, 64 * kh:64 * (kh + 1)],
                                      in0=ptk[:, 64 * kh:64 * (kh + 1)],
                                      scalar1=rc4[:, kh:kh + 1], scalar2=None,
                                      op0=ALU.mult))
            t2 = SV.tok
            v.wait_ge(SV.h, t2)
            for h, (dst, cof) in enumerate([(tkbA, 0), (tkbB, 0), (tkbA, 64), (tkbB, 64)]):
                kh = h // 2
                SV.do(v.tensor_scalar(out=dst[:, cof:cof + 64],
                                      in0=tkpair[:, 64 * kh:64 * (kh + 1)],
                                      scalar1=packTs[:, 8 + h:9 + h], scalar2=None,
                                      op0=ALU.mult))
                SV.do(v.tensor_scalar(out=tkbar[pb][:, 64 * h:64 * (h + 1)],
                                      in0=tkpair[:, 64 * kh:64 * (kh + 1)],
                                      scalar1=wc4[:, h:h + 1], scalar2=None,
                                      op0=ALU.mult))
            SV.do(v.tensor_tensor(out=nbeb[pb][:, :], in0=packTs[:, 8:12], in1=eB[:, :],
                                  op=ALU.mult))
            t3 = SV.tok
            v.wait_ge(SV.h, t3)
            SV.do(v.tensor_scalar(out=nbeb[pb][:, :], in0=nbeb[pb][:, :], scalar1=-1.0,
                                  scalar2=None, op0=ALU.mult))
            for h in range(4):
                kh = h // 2
                SV.do(v.tensor_scalar(out=s2c[pb][:, h:h + 1], in0=rc4[:, 2 + kh:3 + kh],
                                      scalar1=SCALE, scalar2=None, op0=ALU.mult))
            t4 = SV.tok
            v.wait_ge(SV.h, t4)
            for h in range(4):
                SV.do(v.tensor_scalar(out=rhsb[pb][h][:, :], in0=tvs[pb][h][:, :],
                                      scalar1=packTs[:, 8 + h:9 + h], scalar2=None,
                                      op0=ALU.mult))
            SV.do(v.tensor_tensor(out=s1c[pb][:, :], in0=s2c[pb][:, :], in1=eB[:, :],
                                  op=ALU.mult))
        c["V4b"] = SV.tok

        @block.tensor
        def _(t):
            t.wait_ge(SV.h, c["V4b"])
            STN.do(t.transpose(pkh[:, :], tkpair[:, :], IDn[:, :]))
            STN.do(t.transpose(pkb[0][:, :], tkbA[:, :], IDn[:, :]))
            STN.do(t.transpose(pkb[1][:, :], tkbB[:, :], IDn[:, :]))
        c["T5a"] = STN.tok

        @block.scalar
        def _(sc):
            sc.wait_ge(SS.h, SS.tok)
            sc.wait_ge(STN.h, c["T5a"])
            if ch >= 2:
                sc.wait_ge(STN.h, C[ch - 1]["T5g"])
            SS.do(sc.copy(out=khs[pb][:, :], in_=pkh[:, :]))
            SS.do(sc.copy(out=khb02[pb][:, :], in_=pkb[0][:, :]))
            SS.do(sc.copy(out=khb13[pb][:, :], in_=pkb[1][:, :]))
        c["S5b"] = SS.tok

        @block.tensor
        def _(t):
            t.wait_ge(SS.h, c["S5b"])
            for h in range(4):
                r0 = 64 * (h // 2)
                q = slice(128 * h, 128 * (h + 1))
                khbs = khb02[pb] if h % 2 == 0 else khb13[pb]
                rb = 64 * (h // 2)
                STN.do(t.matmul(pg1[:, q], khbs[rb:rb + 64, :], khs[pb][r0:r0 + 64, :],
                                start=True, stop=True))
                STN.do(t.matmul(pg2[:, q], khs[pb][r0:r0 + 64, :], khbs[rb:rb + 64, :],
                                start=True, stop=True))
                STN.do(t.matmul(pg3[:, q], khs[pb][r0:r0 + 64, :],
                                qkvb[sl][0][r0:r0 + 64, csl], start=True, stop=True))
        c["T5g"] = STN.tok

        @block.vector
        def _(v):
            v.wait_ge(SV.h, SV.tok)
            v.wait_ge(STN.h, c["T5g"])
            for h in range(4):
                q = slice(128 * h, 128 * (h + 1))
                SV.do(v.tensor_tensor(out=anat[h][:, :], in0=dnat[h][:, :],
                                      in1=pg1[:, q], op=ALU.mult))
                SV.do(v.tensor_tensor(out=att_[h][:, :], in0=dtt[h][:, :],
                                      in1=pg2[:, q], op=ALU.mult))
                SV.do(v.tensor_tensor(out=attq[pb][h][:, :], in0=dtt[h][:, :],
                                      in1=pg3[:, q], op=ALU.mult))
            t1 = SV.tok
            v.wait_ge(SV.h, t1)
            for h in range(4):
                SV.do(v.tensor_tensor(out=anat[h][:, :], in0=anat[h][:, :],
                                      in1=g["mask_lsn"][:, :], op=ALU.mult))
                SV.do(v.tensor_tensor(out=att_[h][:, :], in0=att_[h][:, :],
                                      in1=g["mask_usn"][:, :], op=ALU.mult))
            t2 = SV.tok
            v.wait_ge(SV.h, t2)
            for h in range(4):
                SV.do(v.tensor_copy(pbuf[pb][h][:, :], att_[h][:, :]))
        c["V6"] = SV.tok

        tok_Vd = c["V6"]
        cur_n, cur_t = anat, att_
        for lev in range(1, 7):
            @block.tensor
            def _(t, cur_n=cur_n, cur_t=cur_t, lev=lev, tok_Vd=tok_Vd):
                t.wait_ge(SV.h, tok_Vd)
                if lev >= 2:
                    t.wait_ge(SS.h, c[f"Sd{lev - 1}"])
                    t.wait_ge(SV.h, c[f"Vt{lev - 1}"])
                for h in range(4):
                    q = slice(128 * h, 128 * (h + 1))
                    STN.do(t.matmul(pdd[0][:, q], cur_t[h][:, :], cur_n[h][:, :],
                                    start=True, stop=True))
                    STN.do(t.matmul(pdd[1][:, q], cur_n[h][:, :], cur_t[h][:, :],
                                    start=True, stop=True))
            tok_Ta = STN.tok

            new_n = [nnb[h][lev % 2] for h in range(4)]
            new_t = [ntb[h][lev % 2] for h in range(4)]

            @block.scalar
            def _(sc, new_n=new_n, tok_Ta=tok_Ta):
                sc.wait_ge(SS.h, SS.tok)
                sc.wait_ge(STN.h, tok_Ta)
                for h in range(4):
                    q = slice(128 * h, 128 * (h + 1))
                    SS.do(sc.copy(out=new_n[h][:, :], in_=pdd[0][:, q]))
            c[f"Sd{lev}"] = SS.tok

            @block.vector
            def _(v, new_t=new_t, tok_Ta=tok_Ta):
                v.wait_ge(SV.h, SV.tok)
                v.wait_ge(STN.h, tok_Ta)
                for h in range(4):
                    q = slice(128 * h, 128 * (h + 1))
                    SV.do(v.tensor_copy(new_t[h][:, :], pdd[1][:, q]))
            c[f"Vt{lev}"] = SV.tok

            @block.tensor
            def _(t, new_n=new_n, new_t=new_t, lev=lev):
                t.wait_ge(SS.h, c[f"Sd{lev}"])
                t.wait_ge(SV.h, c[f"Vt{lev}"])
                for h in range(4):
                    q = slice(128 * h, 128 * (h + 1))
                    STN.do(t.matmul(ppr[:, q], new_n[h][:, :], pbuf[pb][h][:, :],
                                    start=True, stop=False))
                    STN.do(t.matmul(ppr[:, q], IDn[:, :], new_t[h][:, :],
                                    start=False, stop=True))
            tok_Tb = STN.tok

            @block.vector
            def _(v, tok_Tb=tok_Tb):
                v.wait_ge(SV.h, SV.tok)
                v.wait_ge(STN.h, tok_Tb)
                for h in range(4):
                    q = slice(128 * h, 128 * (h + 1))
                    SV.do(v.tensor_tensor(out=pbuf[pb][h][:, :], in0=pbuf[pb][h][:, :],
                                          in1=ppr[:, q], op=ALU.add))
            tok_Vd = SV.tok
            cur_n, cur_t = new_n, new_t
        c["Vd"] = tok_Vd

    def main_chunk(ch):
        G, c4 = ch // 4, ch % 4
        sl = G % 2
        pb = ch % 2
        csl = slice(128 * c4, 128 * (c4 + 1))
        c = C[ch]

        @block.tensor
        def _(t):
            t.wait_ge(SV.h, C[ch - 1]["V12s"] if ch >= 1 else 0)
            t.wait_ge(SS.h, C[ch - 1]["S14"] if ch >= 1 else 0)
            t.wait_ge(SS.h, c["S5b"])
            for hh in ([0, 1], [2, 3]):
                for h in hh:
                    r0 = 64 * (h // 2)
                    q = slice(128 * h, 128 * (h + 1))
                    STN.do(t.matmul(pks[:, q], khs[pb][r0:r0 + 64, :],
                                    SAB[h % 2][r0:r0 + 64, :], start=True, stop=True))
                for h in hh:
                    r0 = 64 * (h // 2)
                    q = slice(128 * h, 128 * (h + 1))
                    STN.do(t.matmul(po1[:, q], qkvb[sl][0][r0:r0 + 64, csl],
                                    SAB[h % 2][r0:r0 + 64, :], start=True, stop=True))
            tok_qdone[ch] = STN.tok
        c["T5m"] = STN.tok

        @block.vector
        def _(v):
            v.wait_ge(SV.h, SV.tok)
            v.wait_ge(STN.h, c["T5m"])
            for h in range(4):
                q = slice(128 * h, 128 * (h + 1))
                SV.do(v.scalar_tensor_tensor(out=rhsb[pb][h][:, :], in0=pks[:, q],
                                             scalar=nbeb[pb][:, h:h + 1],
                                             in1=rhsb[pb][h][:, :],
                                             op0=ALU.mult, op1=ALU.add))
        c["V8f"] = SV.tok

        @block.tensor
        def _(t):
            t.wait_ge(SV.h, c["V8f"])
            t.wait_ge(SV.h, c["Vd"])
            for h in range(4):
                q = slice(128 * h, 128 * (h + 1))
                STN.do(t.matmul(pdl[:, q], pbuf[pb][h][:, :], rhsb[pb][h][:, :],
                                start=True, stop=True))
        c["T9"] = STN.tok

        @block.vector
        def _(v):
            v.wait_ge(SV.h, SV.tok)
            v.wait_ge(STN.h, c["T9"])
            for h in range(4):
                q = slice(128 * h, 128 * (h + 1))
                SV.do(v.tensor_tensor(out=delt[h][:, :], in0=rhsb[pb][h][:, :],
                                      in1=pdl[:, q], op=ALU.add))
        c["V10"] = SV.tok

        @block.tensor
        def _(t):
            t.wait_ge(SV.h, c["V10"])
            for h in range(4):
                q = slice(128 * h, 128 * (h + 1))
                STN.do(t.matmul(po2[:, q], attq[pb][h][:, :], delt[h][:, :],
                                start=True, stop=True))
        c["T11"] = STN.tok

        @block.vector
        def _(v):
            v.wait_ge(SV.h, SV.tok)
            v.wait_ge(STN.h, c["T11"])
            for h in range(4):
                q = slice(128 * h, 128 * (h + 1))
                SV.do(v.tensor_scalar(out=obuf[h][:, :], in0=po1[:, q],
                                      scalar1=s1c[pb][:, h:h + 1], scalar2=None,
                                      op0=ALU.mult))
            t1 = SV.tok
            v.wait_ge(SV.h, t1)
            for h in range(4):
                q = slice(128 * h, 128 * (h + 1))
                SV.do(v.scalar_tensor_tensor(out=obuf[h][:, :], in0=po2[:, q],
                                             scalar=s2c[pb][:, h:h + 1],
                                             in1=obuf[h][:, :],
                                             op0=ALU.mult, op1=ALU.add))
        c["V12a"] = SV.tok

        @block.tensor
        def _(t):
            t.wait_ge(SV.h, c["V12a"])
            for h in range(4):
                r0 = 64 * (h // 2)
                STN.do(t.matmul(puu[h % 2][r0:r0 + 64, :],
                                tkbar[pb][:, 64 * h:64 * (h + 1)],
                                delt[h][:, :], start=True, stop=True,
                                tile_position=(0, r0)))
        c["T11b"] = STN.tok

        @block.scalar
        def _(sc):
            sc.wait_ge(SS.h, SS.tok)
            sc.wait_ge(SV.h, c["V12a"])
            for h in range(4):
                t0 = SS.tok
                if h > 0:
                    sc.wait_ge(SS.h, t0)
                SS.do(sc.activation(out=osq[:, :], in_=obuf[h][:, :], func=AF.Square,
                                    accum_out=rmsc[:, h:h + 1]))
            t1 = SS.tok
            sc.wait_ge(SS.h, t1)
            SS.do(sc.activation(out=rq4[:, :], in_=rmsc[:, :], func=AF.Sqrt,
                                bias=MISC[:, 1:2], scale=1.0 / DV))
        c["S12"] = SS.tok

        @block.vector
        def _(v):
            v.wait_ge(SV.h, SV.tok)
            v.wait_ge(STN.h, c["T11b"])
            for h in range(4):
                r0 = 64 * (h // 2)
                SX = SAB[h % 2]
                SV.do(v.scalar_tensor_tensor(out=SX[r0:r0 + 64, :], in0=SX[r0:r0 + 64, :],
                                             scalar=eBCb[pb][r0:r0 + 64, h:h + 1],
                                             in1=puu[h % 2][r0:r0 + 64, :],
                                             op0=ALU.mult, op1=ALU.add))
        c["V12s"] = SV.tok

        @block.vector
        def _(v):
            v.wait_ge(SV.h, SV.tok)
            v.wait_ge(SS.h, c["S12"])
            SV.do(v.reciprocal(out=rr4[:, :], in_=rq4[:, :]))
            t1 = SV.tok
            v.wait_ge(SV.h, t1)
            for h in range(4):
                SV.do(v.tensor_scalar(out=obuf[h][:, :], in0=obuf[h][:, :],
                                      scalar1=rr4[:, h:h + 1], scalar2=None,
                                      op0=ALU.mult))
        c["V12b"] = SV.tok

        @block.tensor
        def _(t):
            t.wait_ge(SV.h, c["V12b"])
            for h in range(4):
                q = slice(128 * h, 128 * (h + 1))
                STN.do(t.transpose(pot[:, q], obuf[h][:, :], IDn[:, :]))
        c["T13"] = STN.tok

        osl = (ch % 2) * 4
        @block.scalar
        def _(sc, osl=osl):
            sc.wait_ge(SS.h, SS.tok)
            sc.wait_ge(STN.h, c["T13"])
            j = ch // 8
            cof = (ch % 8) * 128
            toks = {}
            for h in range(4):
                q = slice(128 * h, 128 * (h + 1))
                sc.wait_ge(sO[osl + h].h, sO[osl + h].n)
                toks[h] = SS.do(sc.copy(out=oship[osl + h][:, :], in_=pot[:, q]))
            for h in range(4):
                sc.wait_ge(SS.h, toks[h])
                sO[osl + h].inc(sc.dma_start(out=cc2in[j, 128 * h:128 * (h + 1), cof:cof + 128],
                                             in_=oship[osl + h][:, :]), 16)
        c["S14"] = SS.tok

    NCHN = 4 * NG
    for ch in range(NCHN):
        if ch % 4 == 0:
            group_sections(ch // 4)
        prep_chunk(ch)
        if wdprep is not None:
            wdprep(block, ch)
        if ch >= 1:
            main_chunk(ch - 1)
    main_chunk(NCHN - 1)

    @block.gpsimd
    def _(gp):
        for h in range(8):
            gp.wait_ge(sO[h].h, sO[h].n)
        gp.wait_ge(SV.h, SV.tok)
        gp.wait_ge(SS.h, SS.tok)
        gp.wait_ge(STN.h, STN.tok)

    return dict(sO=sO)


P5_MAX = 99
C1 = float(1.0 / np.sqrt(2048.0))
C2 = float(1.0 / np.sqrt(8192.0))


def tile_lhsT(W, nit):
    nin = W.shape[1]
    A = W.reshape(nit, 128, nin // 128, 128)
    A = np.ascontiguousarray(A.transpose(0, 3, 2, 1))
    return A.reshape(nit, 128, nin)


def emit_phase5(nc, block, ex, g, cc2out, z_scr, xn_scr, x_in, y_out,
                WO2, WU2, WG2, WD2, vec, entry=None):
    NTC = TPC // 256
    buf1 = [sbuf(nc, ex, f"p5b1_{f}", [128, 256]) for f in range(16)]
    xhB = [sbuf(nc, ex, f"p5xb{f}", [128, 256]) for f in range(16)]
    ub = [sbuf(nc, ex, f"p5ub{i}", [128, 256]) for i in range(64)]
    sres = [sbuf(nc, ex, f"p5sr{f}", [128, 256]) for f in range(16)]
    wsl = [sbuf(nc, ex, f"p5w{i}", [128, 2048]) for i in range(4)]
    zsl = [sbuf(nc, ex, f"p5z{i}", [128, 256]) for i in range(2)]
    sgt = [sbuf(nc, ex, f"p5sg{i}", [128, 256]) for i in range(2)]
    ut = [sbuf(nc, ex, f"p5ut{i}", [128, 256]) for i in range(2)]
    btmp = [sbuf(nc, ex, f"p5bt{i}", [128, 256]) for i in range(4)]
    outst = [sbuf(nc, ex, f"p5o{i}", [128, 2048]) for i in range(2)]
    xld = [sbuf(nc, ex, f"p5x{i}", [128, 2048]) for i in range(2)]
    sut = sbuf(nc, ex, "p5su", [128, 16])
    sudpt = sbuf(nc, ex, "p5sd", [128, 64])
    wsut = sbuf(nc, ex, "p5wu", [128, 64])
    wsgt = sbuf(nc, ex, "p5wg", [128, 64])
    wsdt = sbuf(nc, ex, "p5wd", [128, 16])
    nwt = sbuf(nc, ex, "p5nw", [128, 1])
    wsuC2 = sbuf(nc, ex, "p5wc2", [128, 64])
    wsgC = sbuf(nc, ex, "p5wgc", [128, 64])
    wsdC = sbuf(nc, ex, "p5wdc", [128, 16])
    PB = [psum(nc, ex, f"p5P{i}", [128, 512]) for i in range(8)]
    pAb = [PB[0], PB[1]]
    pBb = [PB[2], PB[3]]
    pUb = [PB[0], PB[1]]
    pGb = [PB[2], PB[3]]
    pFb = [PB[4], PB[5]]
    pDb = [PB[4], PB[5]]
    pT = PB[6]

    SV = ES(nc, ex, "p5SV")
    SS = ES(nc, ex, "p5SS")
    STN = ES(nc, ex, "p5ST")
    sW = [Sem(nc, ex, f"p5sw{i}") for i in range(4)]
    sOZ = [Sem(nc, ex, f"p5oz{i}") for i in range(2)]
    sGD = Sem(nc, ex, "p5gd")
    sXN = Sem(nc, ex, "p5xn")
    sXL = [Sem(nc, ex, f"p5xl{i}") for i in range(2)]
    sST = [Sem(nc, ex, f"p5st{i}") for i in range(2)]
    sPRE = Sem(nc, ex, "p5pre")

    IDn = g["ident"]

    @block.sync
    def _(sync):
        if entry is not None:
            sync.wait_ge(entry[0], entry[1])
        for src_ap, dst in [(vec["su_t"], sut), (vec["sudp_t"], sudpt),
                            (vec["wsu_t"], wsut), (vec["wsg_t"], wsgt),
                            (vec["wsd_t"], wsdt), (vec["nw"], nwt)]:
            sPRE.inc(sync.dma_start(out=dst[:, :], in_=src_ap[:, :]), 16)

    @block.vector
    def _(v):
        v.wait_ge(sPRE.h, sPRE.n)
        SV.do(v.tensor_scalar(out=wsuC2[:, :], in0=wsut[:, :], scalar1=C1,
                              scalar2=None, op0=ALU.mult))
        t1 = SV.tok
        v.wait_ge(SV.h, t1)
        SV.do(v.tensor_tensor(out=wsuC2[:, :], in0=wsuC2[:, :], in1=sudpt[:, :],
                              op=ALU.mult))
        SV.do(v.tensor_scalar(out=wsgC[:, :], in0=wsgt[:, :], scalar1=C1,
                              scalar2=None, op0=ALU.mult))
        SV.do(v.tensor_scalar(out=wsdC[:, :], in0=wsdt[:, :], scalar1=C2,
                              scalar2=None, op0=ALU.mult))
    tok_vprep = SV.tok

    tok_chunk_T = {-1: 0}
    tok_chunk_V = {-1: 0}
    tok_chunk_S = {-1: 0}
    st_w = [0, 0, 0, 0]
    tok_outev = [0, 0]

    def emit_chunk(b, tc):
        ci = b * NTC + tc
        csl = slice(256 * tc, 256 * (tc + 1))

        @block.sync
        def _(sync):
            sync.wait_ge(STN.h, tok_chunk_T[ci - 1])
            sync.wait_ge(SV.h, tok_chunk_V[ci - 1])
            for f in range(16):
                sGD.inc(sync.dma_start(
                    out=buf1[f][:, :],
                    in_=cc2out[4 * b + f // 4, 128 * (f % 4):128 * (f % 4 + 1), csl]), 16)
        tok_oload = sGD.n

        if P5_MAX < 1:
            tok_chunk_T[ci] = STN.tok; tok_chunk_V[ci] = SV.tok; tok_chunk_S[ci] = SS.tok; return
        tok_zg = {}
        for f in range(16):
            @block.sync
            def _(sync, f=f):
                if f >= 2:
                    sync.wait_ge(SV.h, tok_zg[f - 2])
                else:
                    sync.wait_ge(SV.h, tok_chunk_V[ci - 1])
                sOZ[f % 2].inc(sync.dma_start(out=zsl[f % 2][:, :],
                                              in_=z_scr[b, f, :, csl]), 16)

            @block.scalar
            def _(sc, f=f):
                sc.wait_ge(SS.h, SS.tok)
                sc.wait_ge(sOZ[f % 2].h, sOZ[f % 2].n)
                if f >= 2:
                    sc.wait_ge(SV.h, tok_zg[f - 2])
                else:
                    sc.wait_ge(SV.h, tok_chunk_V[ci - 1])
                if f == 0 and ci == 0:
                    sc.wait_ge(sPRE.h, sPRE.n)
                SS.do(sc.activation(out=sgt[f % 2][:, :], in_=zsl[f % 2][:, :],
                                    func=AF.Silu))
            tok_sz = SS.tok

            @block.vector
            def _(v, f=f, tok_sz=tok_sz):
                v.wait_ge(SV.h, SV.tok)
                v.wait_ge(SS.h, tok_sz)
                if f == 0:
                    v.wait_ge(sGD.h, tok_oload)
                    v.wait_ge(SV.h, tok_vprep)
                SV.do(v.scalar_tensor_tensor(out=buf1[f][:, :], in0=buf1[f][:, :],
                                             scalar=nwt[:, :], in1=sgt[f % 2][:, :],
                                             op0=ALU.mult, op1=ALU.mult))
            tok_zg[f] = SV.tok
        tok_gated = SV.tok

        if P5_MAX < 2:
            tok_chunk_T[ci] = STN.tok; tok_chunk_V[ci] = SV.tok; tok_chunk_S[ci] = SS.tok; return
        tok_opev = {}
        for j in range(16):
            slot = j % 2

            @block.sync
            def _(sync, j=j, slot=slot):
                sync.wait_ge(STN.h, st_w[slot])
                sW[slot].inc(sync.dma_start(out=wsl[slot][:, :], in_=WO2[j, :, :]), 16)

            @block.tensor
            def _(t, j=j, slot=slot):
                if j == 0:
                    t.wait_ge(SV.h, tok_gated)
                    t.wait_ge(SS.h, tok_chunk_S[ci - 1])
                    if ci == 0:
                        t.wait_ge(g["_sem"].h, g["_sem"].n)
                if j >= 2:
                    t.wait_ge(SS.h, tok_opev[j - 2])
                t.wait_ge(sW[slot].h, sW[slot].n)
                mm = None
                for k in range(16):
                    mm = t.matmul(pAb[j % 2][:, 0:256],
                                  wsl[slot][:, k * 128:(k + 1) * 128],
                                  buf1[k][:, :], start=(k == 0), stop=(k == 15))
                STN.do(mm)
                st_w[slot] = STN.tok

            @block.scalar
            def _(sc, j=j):
                sc.wait_ge(SS.h, SS.tok)
                sc.wait_ge(STN.h, STN.tok)
                SS.do(sc.copy(out=sres[j][:, :], in_=pAb[j % 2][:, 0:256]))
            tok_opev[j] = SS.tok
        tok_opT = STN.tok

        if P5_MAX < 3:
            tok_chunk_T[ci] = STN.tok; tok_chunk_V[ci] = SV.tok; tok_chunk_S[ci] = SS.tok; return
        @block.sync
        def _(sync):
            sync.wait_ge(STN.h, tok_opT)
            for f in range(16):
                sXN.inc(sync.dma_start(out=buf1[f][:, :], in_=xn_scr[b, f, :, csl]), 16)
        tok_xnl = sXN.n

        @block.vector
        def _(v):
            v.wait_ge(SV.h, SV.tok)
            v.wait_ge(sXN.h, tok_xnl)
            for f in range(16):
                SV.do(v.tensor_scalar(out=buf1[f][:, :], in0=buf1[f][:, :],
                                      scalar1=sut[:, f:f + 1], scalar2=None,
                                      op0=ALU.mult))
        tok_sumul = SV.tok

        tok_h1ev = {}
        for f in range(16):
            @block.tensor
            def _(t, f=f):
                if f == 0:
                    t.wait_ge(SV.h, tok_sumul)
                if f >= 2:
                    t.wait_ge(SV.h, tok_h1ev[f - 2])
                STN.do(t.matmul(pBb[f % 2][:, 0:256],
                                g["h128"][:, :], buf1[f][:, :], start=True, stop=True))
            tok_h1mm = STN.tok

            @block.vector
            def _(v, f=f, tok_h1mm=tok_h1mm):
                v.wait_ge(SV.h, SV.tok)
                v.wait_ge(STN.h, tok_h1mm)
                SV.do(v.tensor_copy(xhB[f][:, :], pBb[f % 2][:, 0:256]))
            tok_h1ev[f] = SV.tok
        tok_h1mm_last = STN.tok

        @block.vector
        def _(v):
            v.wait_ge(SV.h, SV.tok)
            v.wait_ge(STN.h, tok_h1mm_last)
            cur, nxt = xhB, buf1
            for hh in (1, 2, 4, 8):
                for i in range(16):
                    prt = i ^ hh
                    if i & hh:
                        SV.do(v.tensor_tensor(out=nxt[i][:, :], in0=cur[prt][:, :],
                                              in1=cur[i][:, :], op=ALU.subtract))
                    else:
                        SV.do(v.tensor_tensor(out=nxt[i][:, :], in0=cur[i][:, :],
                                              in1=cur[prt][:, :], op=ALU.add))
                tb = SV.tok
                v.wait_ge(SV.h, tb)
                cur, nxt = nxt, cur
        tok_xh = SV.tok

        if P5_MAX < 4:
            tok_chunk_T[ci] = STN.tok; tok_chunk_V[ci] = SV.tok; tok_chunk_S[ci] = SS.tok; return
        tok_sg = {}
        tok_ue = {}
        tok_stt = {}
        tok_f2 = {}
        for it in range(65):
            if it < 64:
                @block.sync
                def _(sync, it=it):
                    sync.wait_ge(STN.h, st_w[it % 2])
                    sW[it % 2].inc(sync.dma_start(out=wsl[it % 2][:, :],
                                                  in_=WU2[it, :, :]), 16)
                    sync.wait_ge(STN.h, st_w[2 + it % 2])
                    sW[2 + it % 2].inc(sync.dma_start(out=wsl[2 + it % 2][:, :],
                                                      in_=WG2[it, :, :]), 16)

            @block.tensor
            def _(t, it=it):
                if it >= 1:
                    t.wait_ge(SV.h, tok_stt[it - 1])
                    if it >= 3:
                        t.wait_ge(SV.h, tok_ue[it - 3])
                    STN.do(t.matmul(pFb[(it - 1) % 2][:, 0:256], g["h128"][:, :],
                                    ut[(it - 1) % 2][:, :], start=True, stop=True))
                    tok_f2[it - 1] = STN.tok
                if it < 64:
                    if it == 0:
                        t.wait_ge(SV.h, tok_xh)
                    if it < 2:
                        t.wait_ge(SS.h, tok_opev[15])
                    if it >= 2:
                        t.wait_ge(SS.h, tok_sg[it - 2])
                        t.wait_ge(SV.h, tok_stt[it - 2])
                    t.wait_ge(sW[it % 2].h, sW[it % 2].n)
                    t.wait_ge(sW[2 + it % 2].h, sW[2 + it % 2].n)
                    mm = None
                    for k in range(16):
                        mm = t.matmul(pUb[it % 2][:, 0:256], wsl[it % 2][:, k * 128:(k + 1) * 128],
                                      xhB[k][:, :], start=(k == 0), stop=(k == 15))
                    STN.do(mm)
                    mm = None
                    for k in range(16):
                        mm = t.matmul(pGb[it % 2][:, 0:256], wsl[2 + it % 2][:, k * 128:(k + 1) * 128],
                                      xhB[k][:, :], start=(k == 0), stop=(k == 15))
                    STN.do(mm)
                    st_w[it % 2] = STN.tok
                    st_w[2 + it % 2] = STN.tok
            tok_ugT = STN.tok

            if it < 64:
                @block.scalar
                def _(sc, it=it, tok_ugT=tok_ugT):
                    sc.wait_ge(SS.h, SS.tok)
                    sc.wait_ge(STN.h, tok_ugT)
                    SS.do(sc.activation(out=sgt[it % 2][:, :], in_=pGb[it % 2][:, 0:256],
                                        func=AF.Silu, scale=wsgC[:, it:it + 1]))
                tok_sg[it] = SS.tok

                @block.vector
                def _(v, it=it):
                    v.wait_ge(SV.h, SV.tok)
                    v.wait_ge(SS.h, tok_sg[it])
                    if it >= 2:
                        v.wait_ge(STN.h, tok_f2[it - 2])
                    SV.do(v.scalar_tensor_tensor(out=ut[it % 2][:, :],
                                                 in0=pUb[it % 2][:, 0:256],
                                                 scalar=wsuC2[:, it:it + 1],
                                                 in1=sgt[it % 2][:, :],
                                                 op0=ALU.mult, op1=ALU.mult))
                tok_stt[it] = SV.tok

            if it >= 1:
                @block.vector
                def _(v, it=it):
                    v.wait_ge(SV.h, SV.tok)
                    v.wait_ge(STN.h, tok_f2[it - 1])
                    SV.do(v.tensor_copy(ub[it - 1][:, :], pFb[(it - 1) % 2][:, 0:256]))
                tok_ue[it - 1] = SV.tok

        if P5_MAX < 5:
            tok_chunk_T[ci] = STN.tok; tok_chunk_V[ci] = SV.tok; tok_chunk_S[ci] = SS.tok; return
        tok_ubf = tok_ue[63]

        if P5_MAX < 6:
            tok_chunk_T[ci] = STN.tok; tok_chunk_V[ci] = SV.tok; tok_chunk_S[ci] = SS.tok; return
        tok_dsil = {}
        tok_dprod = {}
        for j in range(16):
            @block.sync
            def _(sync, j=j):
                for q4 in range(4):
                    sync.wait_ge(STN.h, st_w[q4])
                    sW[q4].inc(sync.dma_start(out=wsl[q4][:, :],
                                              in_=WD2[j, :, 2048 * q4:2048 * (q4 + 1)]), 16)

            @block.tensor
            def _(t, j=j):
                if j == 0:
                    t.wait_ge(SV.h, tok_ubf)
                if j >= 2:
                    t.wait_ge(SS.h, tok_dsil[j - 2])
                mm = None
                for q4 in range(4):
                    t.wait_ge(sW[q4].h, sW[q4].n)
                    for kk in range(16):
                        k = 16 * q4 + kk
                        mm = t.matmul(pDb[j % 2][:, 0:256], wsl[q4][:, kk * 128:(kk + 1) * 128],
                                      ub[k][:, :], start=(k == 0), stop=(k == 63))
                STN.do(mm)
                for q4 in range(4):
                    st_w[q4] = STN.tok
            tok_dT = STN.tok

            @block.scalar
            def _(sc, j=j, tok_dT=tok_dT):
                sc.wait_ge(SS.h, SS.tok)
                sc.wait_ge(STN.h, tok_dT)
                if j >= 2:
                    sc.wait_ge(SV.h, tok_dprod[j - 2])
                SS.do(sc.activation(out=sgt[j % 2][:, :], in_=pDb[j % 2][:, 0:256],
                                    func=AF.Silu, scale=wsdC[:, j:j + 1]))
            tok_dsil[j] = SS.tok

            @block.vector
            def _(v, j=j):
                v.wait_ge(SV.h, SV.tok)
                v.wait_ge(SS.h, tok_dsil[j])
                SV.do(v.tensor_tensor(out=sres[j][:, :], in0=sres[j][:, :],
                                      in1=sgt[j % 2][:, :], op=ALU.mult))
            tok_dprod[j] = SV.tok
        tok_prod = SV.tok

        if P5_MAX < 7:
            tok_chunk_T[ci] = STN.tok; tok_chunk_V[ci] = SV.tok; tok_chunk_S[ci] = SS.tok; return
        @block.sync
        def _(sync):
            for th in range(2):
                sync.wait_ge(SV.h, tok_outev[th])
                sXL[th].inc(sync.dma_start(
                    out=xld[th][:, :],
                    in_=x_in[b, 256 * tc + 128 * th:256 * tc + 128 * th + 128, :]), 16)

        for th in range(2):
            tok_twave = {}
            tok_vwave = {}
            for wv in range(4):
                @block.tensor
                def _(t, th=th, wv=wv):
                    if wv >= 1:
                        t.wait_ge(SV.h, tok_vwave[wv - 1])
                    elif th == 1:
                        t.wait_ge(SV.h, tok_outev[0])
                    if th == 0 and wv == 0:
                        t.wait_ge(SV.h, tok_prod)
                        t.wait_ge(SV.h, tok_outev[0])
                        t.wait_ge(SV.h, tok_outev[1])
                    for j4 in range(4):
                        j = 4 * wv + j4
                        STN.do(t.transpose(pT[:, 128 * j4:128 * (j4 + 1)],
                                           sres[j][:, 128 * th:128 * th + 128],
                                           IDn[:, :]))
                tok_twave[wv] = STN.tok

                @block.vector
                def _(v, th=th, wv=wv, tok_twave=tok_twave):
                    v.wait_ge(SV.h, SV.tok)
                    v.wait_ge(STN.h, tok_twave[wv])
                    if wv == 0:
                        v.wait_ge(sXL[th].h, sXL[th].n)
                        v.wait_ge(sST[th].h, sST[th].n)
                    for j4 in range(4):
                        j = 4 * wv + j4
                        SV.do(v.tensor_tensor(
                            out=outst[th][:, 128 * j:128 * (j + 1)],
                            in0=pT[:, 128 * j4:128 * (j4 + 1)],
                            in1=xld[th][:, 128 * j:128 * (j + 1)], op=ALU.add))
                tok_vwave[wv] = SV.tok
            tok_outev[th] = SV.tok

            @block.gpsimd
            def _(gp, th=th):
                gp.wait_ge(SV.h, tok_outev[th])
                sST[th].inc(gp.dma_start(
                    out=y_out[b, 256 * tc + 128 * th:256 * tc + 128 * th + 128, :],
                    in_=outst[th][:, :]), 16)

        tok_chunk_T[ci] = STN.tok
        tok_chunk_V[ci] = SV.tok
        tok_chunk_S[ci] = SS.tok

    for b in range(B):
        for tc in range(NTC):
            emit_chunk(b, tc)

    @block.gpsimd
    def _(gp):
        for th in range(2):
            gp.wait_ge(sST[th].h, sST[th].n)

    return dict(sST=sST)

def _build_p1():
    nc = bass.Bass("TRN2", target_bir_lowering=False)
    x_in = nc.dram_tensor("x_in", [B, TPC, D], F32, kind="ExternalInput").ap()
    wT = nc.dram_tensor("wT", [D, IPD], F32, kind="ExternalInput").ap()
    cc1in = nc.dram_tensor("cc1in", [8, SH_CH, TPC], F32, kind="ExternalOutput").ap()
    z_scr = nc.dram_tensor("z_scr", [B, 16, 128, TPC], F32, kind="ExternalOutput").ap()
    xn_scr = nc.dram_tensor("xn_scr", [B, 16, 128, TPC], F32, kind="ExternalOutput").ap()
    with ExitStack() as ex:
        block = ex.enter_context(nc.Block())
        g = load_consts(nc, block, ex, ["ident", "miscc"])
        emit_phase1(nc, block, ex, g, cc1in, z_scr, xn_scr, x_in, wT)
    return nc


def _build_p3():
    nc = bass.Bass("TRN2", target_bir_lowering=False)
    cc1out = nc.dram_tensor("cc1out", [8, SH_CH, TPC], F32, kind="ExternalInput").ap()
    alog4 = nc.dram_tensor("alog4", [4, 1], F32, kind="ExternalInput").ap()
    dtb4 = nc.dram_tensor("dtb4", [4, 1], F32, kind="ExternalInput").ap()
    cwt = nc.dram_tensor("cwt", [128, 24], F32, kind="ExternalInput").ap()
    cc2in = nc.dram_tensor("cc2in", [8, 512, TPC], F32, kind="ExternalOutput").ap()
    with ExitStack() as ex:
        block = ex.enter_context(nc.Block())
        g = load_consts(nc, block, ex, [n for n, _ in CONST_SHAPES])
        emit_phase3(nc, block, ex, g, cc1out, cc2in, alog4, dtb4, cwt)
    return nc


def _build_p5():
    nc = bass.Bass("TRN2", target_bir_lowering=False)
    cc2out = nc.dram_tensor("cc2out", [8, 512, TPC], F32, kind="ExternalInput").ap()
    z_scr = nc.dram_tensor("z_scr", [B, 16, 128, TPC], F32, kind="ExternalInput").ap()
    xn_scr = nc.dram_tensor("xn_scr", [B, 16, 128, TPC], F32, kind="ExternalInput").ap()
    x_in = nc.dram_tensor("x_in", [B, TPC, D], F32, kind="ExternalInput").ap()
    WO2 = nc.dram_tensor("WO2", [16, 128, 2048], F32, kind="ExternalInput").ap()
    WU2 = nc.dram_tensor("WU2", [64, 128, 2048], F32, kind="ExternalInput").ap()
    WG2 = nc.dram_tensor("WG2", [64, 128, 2048], F32, kind="ExternalInput").ap()
    WD2 = nc.dram_tensor("WD2", [16, 128, 8192], F32, kind="ExternalInput").ap()
    vec = {}
    for n, shp in [("su_t", [128, 16]), ("sudp_t", [128, 64]), ("wsu_t", [128, 64]),
                   ("wsg_t", [128, 64]), ("wsd_t", [128, 16]), ("nw", [128, 1])]:
        vec[n] = nc.dram_tensor(n, shp, F32, kind="ExternalInput").ap()
    y_out = nc.dram_tensor("y_out", [B, TPC, D], F32, kind="ExternalOutput").ap()
    with ExitStack() as ex:
        block = ex.enter_context(nc.Block())
        g = load_consts(nc, block, ex, ["ident", "h128", "miscc"])
        emit_phase5(nc, block, ex, g, cc2out, z_scr, xn_scr, x_in, y_out,
                    WO2, WU2, WG2, WD2, vec)
    return nc


def kernel_unfused(**inputs):
    consts = host_constants()
    x = np.ascontiguousarray(inputs["L0_attn_x"], dtype=np.float32)
    wT = np.ascontiguousarray(inputs["L0_attn_in_proj_w"].T.astype(np.float32))
    convw = np.asarray(inputs["L0_attn_conv_w"], np.float32)
    alog = np.asarray(inputs["L0_attn_A_log"], np.float32)
    dtb = np.asarray(inputs["L0_attn_dt_bias"], np.float32)
    nw = np.asarray(inputs["L0_attn_norm_weight"], np.float32)
    WO2 = tile_lhsT(np.ascontiguousarray(inputs["L0_attn_out_proj_w"], dtype=np.float32), 16)
    WU2 = tile_lhsT(np.ascontiguousarray(inputs["L0_ffn_W_up"], dtype=np.float32), 64)
    WG2 = tile_lhsT(np.ascontiguousarray(inputs["L0_ffn_W_gate"], dtype=np.float32), 64)
    WD2 = tile_lhsT(np.ascontiguousarray(inputs["L0_ffn_W_down"], dtype=np.float32), 16)
    su = np.asarray(inputs["L0_ffn_SU_ug"], np.float32)
    sdp = np.asarray(inputs["L0_ffn_SU_dp"], np.float32)
    wsug = np.asarray(inputs["L0_ffn_Wscale_ug"], np.float32)
    wsdp = np.asarray(inputs["L0_ffn_Wscale_dp"], np.float32)
    vecs = dict(su_t=np.ascontiguousarray(su.reshape(16, 128).T),
                sudp_t=np.ascontiguousarray(sdp.reshape(64, 128).T),
                wsu_t=np.ascontiguousarray(wsug[:INTER].reshape(64, 128).T),
                wsg_t=np.ascontiguousarray(wsug[INTER:].reshape(64, 128).T),
                wsd_t=np.ascontiguousarray(wsdp.reshape(16, 128).T),
                nw=np.ascontiguousarray(nw.reshape(128, 1)))
    cdict = {("c_" + n): consts[n] for n, _ in CONST_SHAPES}

    nc1 = _build_p1()
    im1 = []
    for c in range(NCORES):
        d = {"x_in": np.ascontiguousarray(x[:, c * TPC:(c + 1) * TPC, :]), "wT": wT}
        d.update({k: cdict[k] for k in ("c_ident", "c_miscc")})
        im1.append(d)
    r1 = run_bass_kernel_spmd(nc1, im1, core_ids=list(range(NCORES))).results

    cc1out = [np.empty((8, SH_CH, TPC), np.float32) for _ in range(NCORES)]
    for j in range(NCORES):
        for i in range(NCORES):
            cc1out[j][i] = r1[i]["cc1in"].reshape(8, SH_CH, TPC)[j]

    nc3 = _build_p3()
    im3 = []
    for c in range(NCORES):
        u = c % 4
        cwt = np.zeros((128, 24), np.float32)
        rows = np.concatenate([convw[128 * u:128 * (u + 1)],
                               convw[512 + 128 * u:512 + 128 * (u + 1)],
                               convw[1024 + 512 * u:1024 + 512 * (u + 1)]], axis=0)
        for t in range(6):
            cwt[:, 4 * t:4 * t + 4] = rows[128 * t:128 * (t + 1)]
        d = {"cc1out": cc1out[c],
             "alog4": np.ascontiguousarray(alog[4 * u:4 * u + 4].reshape(4, 1)),
             "dtb4": np.ascontiguousarray(dtb[4 * u:4 * u + 4].reshape(4, 1)),
             "cwt": cwt}
        d.update(cdict)
        im3.append(d)
    r3 = run_bass_kernel_spmd(nc3, im3, core_ids=list(range(NCORES))).results

    cc2out = [np.empty((8, 512, TPC), np.float32) for _ in range(NCORES)]
    for j in range(NCORES):
        for i in range(NCORES):
            cc2out[j][i] = r3[i]["cc2in"].reshape(8, 512, TPC)[j]

    nc5 = _build_p5()
    im5 = []
    for c in range(NCORES):
        d = {"cc2out": cc2out[c],
             "z_scr": r1[c]["z_scr"].reshape(B, 16, 128, TPC),
             "xn_scr": r1[c]["xn_scr"].reshape(B, 16, 128, TPC),
             "x_in": np.ascontiguousarray(x[:, c * TPC:(c + 1) * TPC, :]),
             "WO2": WO2, "WU2": WU2, "WG2": WG2, "WD2": WD2}
        d.update(vecs)
        d.update({k: cdict[k] for k in ("c_ident", "c_h128", "c_miscc")})
        im5.append(d)
    r5 = run_bass_kernel_spmd(nc5, im5, core_ids=list(range(NCORES))).results

    y = np.empty((B, T, D), np.float32)
    for c in range(NCORES):
        y[:, c * TPC:(c + 1) * TPC, :] = r5[c]["y_out"].reshape(B, TPC, D)
    return y


def make_wdprep(nc, block, ex, WD2ap, wd2s_ap):
    wdb = sbuf(nc, ex, "wdpb", [128, 8192])
    wtmp = [sbuf(nc, ex, f"wdpt{i}", [128, 128]) for i in range(4)]
    sWL = Sem(nc, ex, "wdpl")
    sWS = Sem(nc, ex, "wdps")
    GW = ES(nc, ex, "wdpg")

    def blk(i):
        return wdb[:, 128 * i:128 * (i + 1)]

    def stages(gp, hhs):
        for hh in hhs:
            los = [i for i in range(64) if not (i & hh)]
            for bat in range(0, 32, 4):
                quad = los[bat:bat + 4]
                for x, i in enumerate(quad):
                    GW.do(gp.tensor_tensor(out=wtmp[x][:, :], in0=blk(i),
                                           in1=blk(i | hh), op=ALU.subtract))
                for i in quad:
                    GW.do(gp.tensor_tensor(out=blk(i), in0=blk(i),
                                           in1=blk(i | hh), op=ALU.add))
                tw = GW.tok
                gp.wait_ge(GW.h, tw)
                for x, i in enumerate(quad):
                    GW.do(gp.tensor_copy(blk(i | hh), wtmp[x][:, :]))
                tw = GW.tok
                gp.wait_ge(GW.h, tw)

    def cb(block, idx):
        if idx >= 64:
            return
        j, ph = idx // 4, idx % 4

        @block.gpsimd
        def _(gp, j=j, ph=ph):
            if ph == 0:
                gp.wait_ge(GW.h, GW.tok)
                gp.wait_ge(sWS.h, sWS.n)
                sWL.inc(gp.dma_start(out=wdb[:, :], in_=WD2ap[j, :, :]), 16)
            elif ph == 1:
                gp.wait_ge(sWL.h, sWL.n)
                stages(gp, (1, 2))
            elif ph == 2:
                gp.wait_ge(GW.h, GW.tok)
                stages(gp, (4, 8))
            else:
                gp.wait_ge(GW.h, GW.tok)
                stages(gp, (16, 32))
                tw = GW.tok
                gp.wait_ge(GW.h, tw)
                sWS.inc(gp.dma_start(out=wd2s_ap[j, :, :], in_=wdb[:, :]), 16)

    cb.sWS = sWS
    return cb

def _build_fused():
    nc = bass.Bass("TRN2", target_bir_lowering=False)
    x_in = nc.dram_tensor("x_in", [B, TPC, D], F32, kind="ExternalInput").ap()
    wT = nc.dram_tensor("wT", [D, IPD], F32, kind="ExternalInput").ap()
    alog4 = nc.dram_tensor("alog4", [4, 1], F32, kind="ExternalInput").ap()
    dtb4 = nc.dram_tensor("dtb4", [4, 1], F32, kind="ExternalInput").ap()
    cwt = nc.dram_tensor("cwt", [128, 24], F32, kind="ExternalInput").ap()
    WO2 = nc.dram_tensor("WO2", [16, 128, 2048], F32, kind="ExternalInput").ap()
    WU2 = nc.dram_tensor("WU2", [64, 128, 2048], F32, kind="ExternalInput").ap()
    WG2 = nc.dram_tensor("WG2", [64, 128, 2048], F32, kind="ExternalInput").ap()
    WD2 = nc.dram_tensor("WD2", [16, 128, 8192], F32, kind="ExternalInput").ap()
    vec = {}
    for n, shp in [("su_t", [128, 16]), ("sudp_t", [128, 64]), ("wsu_t", [128, 64]),
                   ("wsg_t", [128, 64]), ("wsd_t", [128, 16]), ("nw", [128, 1])]:
        vec[n] = nc.dram_tensor(n, shp, F32, kind="ExternalInput").ap()
    y_out = nc.dram_tensor("y_out", [B, TPC, D], F32, kind="ExternalOutput").ap()
    cc1in_t = nc.dram_tensor("cc1in", [8, SH_CH, TPC], F32)
    cc1out_t = nc.dram_tensor("cc1out", [8, SH_CH, TPC], F32)
    cc2in_t = nc.dram_tensor("cc2in", [8, 512, TPC], F32)
    cc2out_t = nc.dram_tensor("cc2out", [8, 512, TPC], F32)
    wd2s_t = nc.dram_tensor("wd2s", [16, 128, 8192], F32)
    z_scr = nc.dram_tensor("z_scr", [B, 16, 128, TPC], F32).ap()
    xn_scr = nc.dram_tensor("xn_scr", [B, 16, 128, TPC], F32).ap()

    with ExitStack() as topex:
        block = topex.enter_context(nc.Block())
        g = load_consts(nc, block, topex, [n for n, _ in CONST_SHAPES])
        cc1d = Sem(nc, topex, "cc1d")
        cc2d = Sem(nc, topex, "cc2d")

        with ExitStack() as ex1:
            p1 = emit_phase1(nc, block, ex1, g, cc1in_t.ap(), z_scr, xn_scr, x_in, wT)

            @block.gpsimd
            def _(gp):
                for s4 in range(4):
                    gp.wait_ge(p1["sST"][s4].h, p1["sST"][s4].n)
                gp.wait_ge(p1["sXN"].h, p1["sXN"].n)
                with allow_ici_all_to_all():
                    gp.collective_compute(
                        "AllToAll", mybir.AluOpType.bypass,
                        replica_groups=[list(range(8))],
                        ins=[cc1in_t.ap().opt()], outs=[cc1out_t.ap().opt()],
                    ).then_inc(cc1d.h, 1)
            cc1d.n = 1

        with ExitStack() as exW:
            wdprep = make_wdprep(nc, block, exW, WD2, wd2s_t.ap())
            with ExitStack() as ex3:
                p3 = emit_phase3(nc, block, ex3, g, cc1out_t.ap(), cc2in_t.ap(),
                                 alog4, dtb4, cwt, entry=(cc1d.h, 1), wdprep=wdprep)

                @block.gpsimd
                def _(gp):
                    for h in range(8):
                        gp.wait_ge(p3["sO"][h].h, p3["sO"][h].n)
                    gp.wait_ge(wdprep.sWS.h, wdprep.sWS.n)
                    with allow_ici_all_to_all():
                        gp.collective_compute(
                            "AllToAll", mybir.AluOpType.bypass,
                            replica_groups=[list(range(8))],
                            ins=[cc2in_t.ap().opt()], outs=[cc2out_t.ap().opt()],
                        ).then_inc(cc2d.h, 1)
                cc2d.n = 1

        with ExitStack() as ex5:
            p5 = emit_phase5(nc, block, ex5, g, cc2out_t.ap(), z_scr, xn_scr,
                             x_in, y_out, WO2, WU2, WG2, wd2s_t.ap(), vec,
                             entry=(cc2d.h, 1))
    return nc


def kernel(**inputs):
    consts = host_constants()
    x = np.ascontiguousarray(inputs["L0_attn_x"], dtype=np.float32)
    wT = np.ascontiguousarray(inputs["L0_attn_in_proj_w"].T.astype(np.float32))
    convw = np.asarray(inputs["L0_attn_conv_w"], np.float32)
    alog = np.asarray(inputs["L0_attn_A_log"], np.float32)
    dtb = np.asarray(inputs["L0_attn_dt_bias"], np.float32)
    nw = np.asarray(inputs["L0_attn_norm_weight"], np.float32)
    WO2 = tile_lhsT(np.ascontiguousarray(inputs["L0_attn_out_proj_w"], dtype=np.float32), 16)
    WU2 = tile_lhsT(np.ascontiguousarray(inputs["L0_ffn_W_up"], dtype=np.float32), 64)
    WG2 = tile_lhsT(np.ascontiguousarray(inputs["L0_ffn_W_gate"], dtype=np.float32), 64)
    WD2 = tile_lhsT(np.ascontiguousarray(inputs["L0_ffn_W_down"], dtype=np.float32), 16)
    su = np.asarray(inputs["L0_ffn_SU_ug"], np.float32)
    sdp = np.asarray(inputs["L0_ffn_SU_dp"], np.float32)
    wsug = np.asarray(inputs["L0_ffn_Wscale_ug"], np.float32)
    wsdp = np.asarray(inputs["L0_ffn_Wscale_dp"], np.float32)
    vecs = dict(su_t=np.ascontiguousarray(su.reshape(16, 128).T),
                sudp_t=np.ascontiguousarray(sdp.reshape(64, 128).T),
                wsu_t=np.ascontiguousarray(wsug[:INTER].reshape(64, 128).T),
                wsg_t=np.ascontiguousarray(wsug[INTER:].reshape(64, 128).T),
                wsd_t=np.ascontiguousarray(wsdp.reshape(16, 128).T),
                nw=np.ascontiguousarray(nw.reshape(128, 1)))
    cdict = {("c_" + n): consts[n] for n, _ in CONST_SHAPES}

    nc = _build_fused()
    im = []
    for c in range(NCORES):
        u = c % 4
        cwt = np.zeros((128, 24), np.float32)
        rows = np.concatenate([convw[128 * u:128 * (u + 1)],
                               convw[512 + 128 * u:512 + 128 * (u + 1)],
                               convw[1024 + 512 * u:1024 + 512 * (u + 1)]], axis=0)
        for t in range(6):
            cwt[:, 4 * t:4 * t + 4] = rows[128 * t:128 * (t + 1)]
        d = {"x_in": np.ascontiguousarray(x[:, c * TPC:(c + 1) * TPC, :]),
             "wT": wT,
             "alog4": np.ascontiguousarray(alog[4 * u:4 * u + 4].reshape(4, 1)),
             "dtb4": np.ascontiguousarray(dtb[4 * u:4 * u + 4].reshape(4, 1)),
             "cwt": cwt,
             "WO2": WO2, "WU2": WU2, "WG2": WG2, "WD2": WD2}
        d.update(vecs)
        d.update(cdict)
        im.append(d)
    res = run_bass_kernel_spmd(nc, im, core_ids=list(range(NCORES))).results
    y = np.empty((B, T, D), np.float32)
    for c in range(NCORES):
        y[:, c * TPC:(c + 1) * TPC, :] = res[c]["y_out"].reshape(B, TPC, D)
    return y


# version 14
# speedup vs baseline: 1.1776x; 1.0017x over previous
import numpy as np
from contextlib import ExitStack
import concourse.bass as bass
import concourse.mybir as mybir
from concourse.bass_utils import run_bass_kernel_spmd
from concourse.aot_env import allow_ici_all_to_all

F32 = mybir.dt.float32
AF = mybir.ActivationFunctionType
ALU = mybir.AluOpType
AX = mybir.AxisListType

NCORES = 8
B = 2
T = 8192
D = 2048
TPC = T // NCORES
IPD = 5152
NVH, NKH, DK, DV = 16, 8, 64, 128
SCALE = 0.125
SH_CH = 776
C = 128
NCH = T // C
EPS_LN = 1e-5
EPS_RMS = 1e-6
INTER = 8192


class Sem:
    def __init__(self, nc, ex, name):
        cm = nc.semaphore(name)
        self.h = cm.__enter__()
        ex.callback(cm.__exit__, None, None, None)
        self.n = 0

    def inc(self, instr, k=1):
        instr.then_inc(self.h, k)
        self.n += k
        return self.n


class Chain:
    def __init__(self, nc, ex, name):
        self.s = Sem(nc, ex, name)

    def mark(self, instr):
        self.s.inc(instr)
        return instr

    def join(self, eng):
        if self.s.n:
            eng.wait_ge(self.s.h, self.s.n)


def sbuf(nc, ex, name, shape):
    return ex.enter_context(nc.sbuf_tensor(name, shape, F32))


def psum(nc, ex, name, shape):
    return ex.enter_context(nc.psum_tensor(name, shape, F32))


def host_constants():
    c = {}
    c["ident"] = np.eye(128, dtype=np.float32)
    c["ones"] = np.ones((128, 128), dtype=np.float32)
    H = np.array([[1.0]], dtype=np.float32)
    while H.shape[0] < 128:
        H = np.block([[H, H], [H, -H]]).astype(np.float32)
    c["h128"] = np.ascontiguousarray(H)
    idx = np.arange(C)
    c["pen_nat"] = np.where(idx[:, None] < idx[None, :], 1e30, 0.0).astype(np.float32)
    c["pen_t"] = np.where(idx[:, None] > idx[None, :], -1e30, 0.0).astype(np.float32)
    c["mask_lsn"] = np.where(idx[:, None] > idx[None, :], -1.0, 0.0).astype(np.float32)
    c["mask_usn"] = np.where(idx[:, None] < idx[None, :], -1.0, 0.0).astype(np.float32)
    mis = np.zeros((128, 4), dtype=np.float32)
    mis[:, 0] = EPS_LN
    mis[:, 1] = EPS_RMS
    mis[:, 2] = 1.0
    c["miscc"] = mis
    return c


CONST_SHAPES = [("ident", (128, 128)), ("ones", (128, 128)), ("h128", (128, 128)),
                ("pen_nat", (128, 128)), ("pen_t", (128, 128)),
                ("mask_lsn", (128, 128)), ("mask_usn", (128, 128)),
                ("miscc", (128, 4))]


def load_consts(nc, block, ex, names):
    aps, tiles = {}, {}
    for n, shp in CONST_SHAPES:
        if n not in names:
            continue
        aps[n] = nc.dram_tensor("c_" + n, list(shp), F32, kind="ExternalInput").ap()
        tiles[n] = sbuf(nc, ex, "ct_" + n, list(shp))
    sC = Sem(nc, ex, "s_const")

    @block.sync
    def _(sync):
        for n in tiles:
            sC.inc(sync.dma_start(out=tiles[n][:, :], in_=aps[n][:, :]), 16)
    tiles["_sem"] = sC
    return tiles


STRIPS = [(s * 512, 512) for s in range(10)] + [(5120, 32)]


def p1_mm_uses():
    uses = []
    for b in range(B):
        for si, (c0, w) in enumerate(STRIPS):
            for ci in range(max(w // 128, 1)):
                for tc in range(2):
                    uses.append((b, si, ci, tc, (c0 + ci * 128) // 128, w))
    return uses


def emit_phase1(nc, block, ex, g, cc1in, z_scr, xn_scr, x_in, wT):
    NXT = TPC // 128
    xslot = [sbuf(nc, ex, f"p1x{i}", [128, 2048]) for i in range(2)]
    xnt = [sbuf(nc, ex, f"p1n{i}", [128, 2048]) for i in range(2)]
    xnfm = [sbuf(nc, ex, f"p1f{k}", [128, TPC]) for k in range(16)]
    wslot = [sbuf(nc, ex, f"p1w{i}", [128, 16 * 512]) for i in range(2)]
    stage = [sbuf(nc, ex, f"p1s{i}", [128, 512]) for i in range(4)]
    sqt = sbuf(nc, ex, "p1sq", [128, 2048])
    mcol = [sbuf(nc, ex, f"p1m{i}", [128, 1]) for i in range(2)]
    vcol = [sbuf(nc, ex, f"p1v{i}", [128, 1]) for i in range(2)]
    scol = [sbuf(nc, ex, f"p1c{i}", [128, 1]) for i in range(2)]
    rcol = [sbuf(nc, ex, f"p1r{i}", [128, 1]) for i in range(2)]
    ptr = [psum(nc, ex, f"p1pt{i}", [128, 512]) for i in range(2)]
    pmm = [psum(nc, ex, f"p1pm{i}", [128, 512]) for i in range(2)]

    sX = [Sem(nc, ex, "p1sX0"), Sem(nc, ex, "p1sX1")]
    sVar = Sem(nc, ex, "p1sVa")
    sSq = Sem(nc, ex, "p1sSq")
    sLN = Sem(nc, ex, "p1sLN")
    sTR = Sem(nc, ex, "p1sTR")
    sXE = Sem(nc, ex, "p1sXE")
    sW = [Sem(nc, ex, "p1sW0"), Sem(nc, ex, "p1sW1")]
    sMM = Sem(nc, ex, "p1sMM")
    sEV = Sem(nc, ex, "p1sEV")
    sST = [Sem(nc, ex, f"p1sST{i}") for i in range(4)]
    sXN = Sem(nc, ex, "p1sXN")
    vch = Chain(nc, ex, "p1vch")

    mm_uses = p1_mm_uses()
    PERB = len(mm_uses) // B
    strip_last_ui = {}
    for ui, (b, si, ci, tc, gi, w) in enumerate(mm_uses):
        strip_last_ui[b * len(STRIPS) + si] = ui

    def dests(b, tc, gi):
        out = []
        cs = slice(tc * 512, (tc + 1) * 512)
        if gi < 4:
            out.append((cc1in[4 * b + gi, 0:128, cs], (0, 128)))
        elif gi < 8:
            out.append((cc1in[4 * b + gi - 4, 128:256, cs], (0, 128)))
        elif gi < 24:
            u, s4 = (gi - 8) // 4, (gi - 8) % 4
            out.append((cc1in[4 * b + u, 256 + 128 * s4:384 + 128 * s4, cs], (0, 128)))
        elif gi < 40:
            out.append((z_scr[b, gi - 24, :, cs], (0, 128)))
        else:
            for u in range(4):
                out.append((cc1in[4 * b + u, 768:772, cs], (4 * u, 4 * u + 4)))
                out.append((cc1in[4 * b + u, 772:776, cs], (16 + 4 * u, 16 + 4 * u + 4)))
        return out

    st_cum = []
    slot_tot = [0, 0, 0, 0]
    for ui, (b, si, ci, tc, gi, w) in enumerate(mm_uses):
        sl = ui % 4
        slot_tot[sl] += 16 * len(dests(b, tc, gi))
        st_cum.append((sl, slot_tot[sl]))

    @block.sync
    def _(sync):
        for b in range(B):
            for t8 in range(NXT):
                i = b * NXT + t8
                if i >= 2:
                    sync.wait_ge(sLN.h, i - 1)
                sX[i % 2].inc(sync.dma_start(out=xslot[i % 2][:, :],
                                             in_=x_in[b, t8 * 128:(t8 + 1) * 128, :]), 16)
            for si, (c0, w) in enumerate(STRIPS):
                gsi = b * len(STRIPS) + si
                if gsi >= 2:
                    sync.wait_ge(sEV.h, strip_last_ui[gsi - 2] + 1)
                for k in range(16):
                    sW[gsi % 2].inc(sync.dma_start(
                        out=wslot[gsi % 2][:, k * 512:k * 512 + w],
                        in_=wT[k * 128:(k + 1) * 128, c0:c0 + w]), 16)

    @block.vector
    def _(v):
        def tevacs(b, t8):
            for gq in range(4):
                gidx = (b * NXT + t8) * 4 + gq
                v.wait_ge(sTR.h, gidx + 1)
                pt = ptr[gidx % 2]
                ev = None
                for q4 in range(4):
                    ct = gq * 4 + q4
                    ev = v.tensor_copy(xnfm[ct][:, t8 * 128:(t8 + 1) * 128],
                                       pt[:, q4 * 128:(q4 + 1) * 128])
                sXE.inc(ev)

        for b in range(B):
            if b == 1:
                v.wait_ge(sXN.h, 256)
            for t8 in range(NXT):
                i = b * NXT + t8
                s = i % 2
                v.wait_ge(sX[i % 2].h, 16 * (i // 2 + 1))
                if i >= 2:
                    v.wait_ge(sTR.h, 4 * (i - 1))
                xs, xn = xslot[s], xnt[s]
                vch.join(v)
                vch.mark(v.tensor_reduce(mcol[s][:, :], xs[:, :], axis=AX.X, op=ALU.add))
                vch.mark(v.tensor_tensor(out=sqt[:, :], in0=xs[:, :], in1=xs[:, :], op=ALU.mult))
                vch.join(v)
                vch.mark(v.tensor_reduce(vcol[s][:, :], sqt[:, :], axis=AX.X, op=ALU.add))
                vch.mark(v.tensor_scalar(out=mcol[s][:, :], in0=mcol[s][:, :],
                                         scalar1=1.0 / D, scalar2=None, op0=ALU.mult))
                vch.join(v)
                vch.mark(v.tensor_tensor(out=rcol[s][:, :], in0=mcol[s][:, :],
                                         in1=mcol[s][:, :], op=ALU.mult))
                vch.join(v)
                sVar.inc(v.scalar_tensor_tensor(out=vcol[s][:, :], in0=vcol[s][:, :],
                                                scalar=1.0 / D, in1=rcol[s][:, :],
                                                op0=ALU.mult, op1=ALU.subtract))
                v.wait_ge(sSq.h, i + 1)
                vch.mark(v.reciprocal(out=rcol[s][:, :], in_=scol[s][:, :]))
                vch.join(v)
                sLN.inc(v.tensor_scalar(out=xn[:, :], in0=xs[:, :],
                                        scalar1=mcol[s][:, :], scalar2=rcol[s][:, :],
                                        op0=ALU.subtract, op1=ALU.mult))
                if t8 >= 1:
                    tevacs(b, t8 - 1)
            tevacs(b, NXT - 1)
            for ui in range(b * PERB, (b + 1) * PERB):
                (bb, si, ci, tc, gi, w) = mm_uses[ui]
                v.wait_ge(sMM.h, ui + 1)
                if ui >= 4:
                    sl, cum = st_cum[ui - 4]
                    v.wait_ge(sST[sl].h, cum)
                rows = 128 if w >= 128 else 32
                sEV.inc(v.tensor_copy(stage[ui % 4][0:rows, :], pmm[ui % 2][0:rows, :]))

    @block.scalar
    def _(sc):
        for i in range(B * NXT):
            s = i % 2
            sc.wait_ge(sVar.h, i + 1)
            sc.wait_ge(g["_sem"].h, g["_sem"].n)
            sSq.inc(sc.activation(out=scol[s][:, :], in_=vcol[s][:, :],
                                  func=AF.Sqrt, bias=g["miscc"][:, 0:1], scale=1.0))

    @block.tensor
    def _(t):
        t.wait_ge(g["_sem"].h, g["_sem"].n)
        for b in range(B):
            for t8 in range(NXT):
                i = b * NXT + t8
                t.wait_ge(sLN.h, i + 1)
                for gq in range(4):
                    gidx = i * 4 + gq
                    if gidx >= 2:
                        t.wait_ge(sXE.h, gidx - 1)
                    pt = ptr[gidx % 2]
                    mm = None
                    for q4 in range(4):
                        ct = gq * 4 + q4
                        mm = t.transpose(pt[:, q4 * 128:(q4 + 1) * 128],
                                         xnt[i % 2][:, ct * 128:(ct + 1) * 128],
                                         g["ident"][:, :])
                    sTR.inc(mm)
            t.wait_ge(sXE.h, (b + 1) * NXT * 4)
            for ui in range(b * PERB, (b + 1) * PERB):
                (bb, si, ci, tc, gi, w) = mm_uses[ui]
                gsi = b * len(STRIPS) + si
                if ci == 0 and tc == 0:
                    t.wait_ge(sW[gsi % 2].h, 256 * (gsi // 2 + 1))
                if ui >= 2:
                    t.wait_ge(sEV.h, ui - 1)
                ws = wslot[gsi % 2]
                pm = pmm[ui % 2]
                rows = 128 if w >= 128 else 32
                mm = None
                for k in range(16):
                    mm = t.matmul(pm[0:rows, :],
                                  ws[:, k * 512 + ci * 128:k * 512 + ci * 128 + rows],
                                  xnfm[k][:, tc * 512:(tc + 1) * 512],
                                  start=(k == 0), stop=(k == 15))
                sMM.inc(mm)

    @block.gpsimd
    def _(gp):
        for b in range(B):
            for ui in range(b * PERB, (b + 1) * PERB):
                (bb, si, ci, tc, gi, w) = mm_uses[ui]
                gp.wait_ge(sEV.h, ui + 1)
                for ap, (r0, r1) in dests(b, tc, gi):
                    sST[ui % 4].inc(gp.dma_start(out=ap, in_=stage[ui % 4][r0:r1, :]), 16)
            for ct in range(16):
                sXN.inc(gp.dma_start(out=xn_scr[b, ct, :, :], in_=xnfm[ct][:, :]), 16)
        for s4 in range(4):
            gp.wait_ge(sST[s4].h, sST[s4].n)
        gp.wait_ge(sXN.h, sXN.n)

    return dict(sST=sST, sXN=sXN)




P3_MAXSTAGE = 99
P3_SUB = 99

class ES:
    def __init__(self, nc, ex, name):
        self.sem = Sem(nc, ex, name)

    def do(self, instr):
        self.sem.inc(instr, 1)
        return self.sem.n

    @property
    def tok(self):
        return self.sem.n

    @property
    def h(self):
        return self.sem.h


def emit_phase3(nc, block, ex, g, cc1out, cc2in, alog4, dtb4, cwt, ngroups=None,
                entry=None, wdprep=None):
    NG = ngroups if ngroups is not None else T // 512
    convin = [[sbuf(nc, ex, f"p3ci{s}_{t}", [128, 515]) for t in range(6)] for s in range(2)]
    accb = [[sbuf(nc, ex, f"p3ac{s}_{t}", [128, 512]) for t in range(6)] for s in range(2)]
    qkvb = [[sbuf(nc, ex, f"p3qk{s}_{t}", [128, 512]) for t in range(6)] for s in range(2)]
    abb = [sbuf(nc, ex, f"p3ab{s}", [4, 512]) for s in range(2)]
    aba = [sbuf(nc, ex, f"p3aa{s}", [4, 512]) for s in range(2)]
    bet4 = [sbuf(nc, ex, f"p3be{s}", [4, 512]) for s in range(2)]
    g4 = [sbuf(nc, ex, f"p3g4{s}", [4, 512]) for s in range(2)]
    spx = sbuf(nc, ex, "p3spx", [4, 512])
    spe = sbuf(nc, ex, "p3spe", [4, 512])
    ytmp = sbuf(nc, ex, "p3ytm", [4, 512])
    gtmp = sbuf(nc, ex, "p3gt", [4, 128])
    pkB = sbuf(nc, ex, "p3pkB", [4, 128])
    pkNB = sbuf(nc, ex, "p3pkN", [4, 128])
    pkBE = sbuf(nc, ex, "p3pkE", [4, 128])
    brh = [sbuf(nc, ex, f"p3br{h}", [1, 128]) for h in range(4)]
    bcrow = sbuf(nc, ex, "p3bcr", [1, 4])
    packTs = sbuf(nc, ex, "p3pts", [128, 12])
    eB = sbuf(nc, ex, "p3eb", [128, 4])
    wc4 = sbuf(nc, ex, "p3wc", [128, 4])
    brv = sbuf(nc, ex, "p3brv", [128, 4])
    kq4 = sbuf(nc, ex, "p3kq", [128, 4])
    sq4 = sbuf(nc, ex, "p3sq4", [128, 4])
    rc4 = sbuf(nc, ex, "p3rc", [128, 4])
    rq4 = sbuf(nc, ex, "p3rq", [128, 4])
    rr4 = sbuf(nc, ex, "p3rr", [128, 4])
    rmsc = sbuf(nc, ex, "p3rm", [128, 4])
    ksq = sbuf(nc, ex, "p3ksq", [128, 128])
    qsq = sbuf(nc, ex, "p3qsq", [128, 128])
    tkpair = sbuf(nc, ex, "p3tkp", [128, 128])
    tkbA = sbuf(nc, ex, "p3tba", [128, 128])
    tkbB = sbuf(nc, ex, "p3tbb", [128, 128])
    dnat = [sbuf(nc, ex, f"p3dn{h}", [128, 128]) for h in range(4)]
    dtt = [sbuf(nc, ex, f"p3dt{h}", [128, 128]) for h in range(4)]
    anat = [sbuf(nc, ex, f"p3an{h}", [128, 128]) for h in range(4)]
    att_ = [sbuf(nc, ex, f"p3at{h}", [128, 128]) for h in range(4)]
    nnb = [[sbuf(nc, ex, f"p3nn{h}_{i}", [128, 128]) for i in range(2)] for h in range(4)]
    ntb = [[sbuf(nc, ex, f"p3nt{h}_{i}", [128, 128]) for i in range(2)] for h in range(4)]
    delt = [sbuf(nc, ex, f"p3de{h}", [128, 128]) for h in range(4)]
    obuf = [sbuf(nc, ex, f"p3ob{h}", [128, 128]) for h in range(4)]
    osq = [sbuf(nc, ex, f"p3osq{i}", [128, 128]) for i in range(2)]
    oship = [sbuf(nc, ex, f"p3os{h}", [128, 128]) for h in range(8)]
    SAB = [sbuf(nc, ex, f"p3S{i}", [128, 128]) for i in range(2)]
    khs = [sbuf(nc, ex, f"p3khs{p}", [128, 128]) for p in range(2)]
    khb02 = [sbuf(nc, ex, f"p3kb0{p}", [128, 128]) for p in range(2)]
    khb13 = [sbuf(nc, ex, f"p3kb1{p}", [128, 128]) for p in range(2)]
    tkbar = [sbuf(nc, ex, f"p3tbr{p}", [128, 256]) for p in range(2)]
    attq = [[sbuf(nc, ex, f"p3aq{p}_{h}", [128, 128]) for h in range(4)] for p in range(2)]
    pbuf = [[sbuf(nc, ex, f"p3pb{p}_{h}", [128, 128]) for h in range(4)] for p in range(2)]
    rhsb = [[sbuf(nc, ex, f"p3rh{p}_{h}", [128, 128]) for h in range(4)] for p in range(2)]
    tvs = [[sbuf(nc, ex, f"p3tv{p}_{h}", [128, 128]) for h in range(4)] for p in range(2)]
    nbeb = [sbuf(nc, ex, f"p3nb{p}", [128, 4]) for p in range(2)]
    eBCb = [sbuf(nc, ex, f"p3ebc{p}", [128, 4]) for p in range(2)]
    s2c = [sbuf(nc, ex, f"p3s2{p}", [128, 4]) for p in range(2)]
    s1c = [sbuf(nc, ex, f"p3s1{p}", [128, 4]) for p in range(2)]
    eA4 = sbuf(nc, ex, "p3ea", [4, 1])
    dtb_s = sbuf(nc, ex, "p3db", [4, 1])
    alog_s = sbuf(nc, ex, "p3al", [4, 1])
    cw_s = sbuf(nc, ex, "p3cw", [128, 24])
    bankG = psum(nc, ex, "p3bG", [128, 512])
    bankA = psum(nc, ex, "p3bA", [128, 512])
    bankB = psum(nc, ex, "p3bB", [128, 512])
    bankD = psum(nc, ex, "p3bD", [128, 512])
    bankE = psum(nc, ex, "p3bE", [128, 512])
    bankH = psum(nc, ex, "p3bH", [128, 512])
    bankF = psum(nc, ex, "p3bF", [128, 512])
    bankM = psum(nc, ex, "p3bM", [128, 512])
    ptk = bankG[:, 0:128]
    pkh = bankG[:, 128:256]
    pkb = [bankG[:, 256:384], bankG[:, 384:512]]
    ptv = bankA
    ppr = bankA
    pbb = bankB
    pg3 = bankB
    pg1 = bankD
    pdd = [bankE, bankD]
    pg2 = bankE
    ppk = bankH[:, 256:272]
    psm = bankH[:, 272:276]
    pbc = bankH[:, 276:280]
    po1 = bankF
    pks = bankM
    pdl = bankM
    po2 = bankM
    puu = [bankM[:, 0:128], bankM[:, 128:256]]
    pot = bankM

    SV = ES(nc, ex, "p3SV")
    SS = ES(nc, ex, "p3SS")
    STN = ES(nc, ex, "p3ST")
    sCV = [Sem(nc, ex, f"p3cv{i}") for i in range(2)]
    sO = [Sem(nc, ex, f"p3so{h}") for h in range(8)]
    sPRE = Sem(nc, ex, "p3pre")
    sDB1 = Sem(nc, ex, "p3db1")
    sDB2 = Sem(nc, ex, "p3db2")

    IDn = g["ident"]
    ONE = g["ones"]
    MISC = g["miscc"]

    @block.sync
    def _(sync):
        if entry is not None:
            sync.wait_ge(entry[0], entry[1])
        for ap_h, ap_s in [(alog4, alog_s), (dtb4, dtb_s), (cwt, cw_s)]:
            sPRE.inc(sync.dma_start(out=ap_s[:, :], in_=ap_h[:, :]), 16)

    @block.scalar
    def _(sc):
        sc.wait_ge(SS.h, SS.tok)
        sc.wait_ge(sPRE.h, sPRE.n)
        sc.wait_ge(g["_sem"].h, g["_sem"].n)
        SS.do(sc.activation(out=eA4[:, :], in_=alog_s[:, :], func=AF.Exp))

    @block.vector
    def _(v):
        v.wait_ge(SV.h, SV.tok)
        v.wait_ge(g["_sem"].h, g["_sem"].n)
        if entry is not None:
            v.wait_ge(entry[0], entry[1])
        for t in range(6):
            SV.do(v.memset(convin[0][t][:, 0:3], 0.0))
        for i in range(2):
            SV.do(v.memset(SAB[i][:, :], 0.0))

    tok_conv = {}
    tok_silu = {}
    tok_silu_all = {}
    tok_ab = {}
    tok_qdone = {-1: 0}
    C = {}

    def group_sections(G):
        sl = G % 2

        @block.sync
        def _(sync, G=G, sl=sl):
            shard = G // 2
            c0 = (G % 2) * 512
            if G >= 2:
                sync.wait_ge(SV.h, tok_conv[G - 1])
                sync.wait_ge(SV.h, tok_ab[G - 2])
                sync.wait_ge(STN.h, tok_qdone[4 * (G - 2) + 3])
            for t in range(6):
                sCV[sl].inc(sync.dma_start(
                    out=convin[sl][t][:, 3:515],
                    in_=cc1out[shard, 128 * t:128 * (t + 1), c0:c0 + 512]), 16)
            sCV[sl].inc(sync.dma_start(
                out=abb[sl][:, :], in_=cc1out[shard, 768:772, c0:c0 + 512]), 16)
            sCV[sl].inc(sync.dma_start(
                out=aba[sl][:, :], in_=cc1out[shard, 772:776, c0:c0 + 512]), 16)

        @block.vector
        def _(v, G=G, sl=sl):
            v.wait_ge(SV.h, SV.tok)
            v.wait_ge(sCV[sl].h, sCV[sl].n)
            if G >= 1:
                v.wait_ge(sCV[(G - 1) % 2].h, sCV[(G - 1) % 2].n)
                for t in range(6):
                    SV.do(v.tensor_copy(convin[sl][t][:, 0:3],
                                        convin[(G - 1) % 2][t][:, 512:515]))
            tcp = SV.tok
            for t in range(6):
                if G >= 2:
                    v.wait_ge(SS.h, tok_silu[(G - 2, t)])
                if G >= 1:
                    v.wait_ge(SV.h, tcp)
                acc = accb[sl][t]
                ci = convin[sl][t]
                wof = 4 * t
                v.wait_ge(sPRE.h, sPRE.n)
                SV.do(v.tensor_scalar(out=acc[:, :], in0=ci[:, 0:512],
                                      scalar1=cw_s[:, wof:wof + 1], scalar2=None,
                                      op0=ALU.mult))
                for k in range(1, 4):
                    tk = SV.tok
                    v.wait_ge(SV.h, tk)
                    SV.do(v.scalar_tensor_tensor(out=acc[:, :], in0=ci[:, k:k + 512],
                                                 scalar=cw_s[:, wof + k:wof + k + 1],
                                                 in1=acc[:, :], op0=ALU.mult, op1=ALU.add))
            tok_conv[G] = SV.tok

        @block.scalar
        def _(sc, G=G, sl=sl):
            sc.wait_ge(SS.h, SS.tok)
            sc.wait_ge(SV.h, tok_conv[G])
            if G >= 2:
                sc.wait_ge(STN.h, tok_qdone[4 * (G - 2) + 3])
                sc.wait_ge(SS.h, C[4 * (G - 2) + 3]["S14"])
            for t in range(6):
                tok_silu[(G, t)] = SS.do(sc.activation(out=qkvb[sl][t][:, :],
                                                       in_=accb[sl][t][:, :],
                                                       func=AF.Silu))
            tok_silu_all[G] = SS.tok
            sc.wait_ge(sPRE.h, sPRE.n)
            SS.do(sc.activation(out=bet4[sl][:, :], in_=abb[sl][:, :], func=AF.Sigmoid))

        @block.vector
        def _(v, G=G, sl=sl):
            v.wait_ge(SV.h, SV.tok)
            v.wait_ge(sCV[sl].h, sCV[sl].n)
            v.wait_ge(sPRE.h, sPRE.n)
            if G >= 2:
                v.wait_ge(SV.h, tok_ab[G - 2])
                v.wait_ge(SV.h, C[4 * (G - 2) + 3]["V1"])
            SV.do(v.tensor_scalar(out=ytmp[:, :], in0=aba[sl][:, :],
                                  scalar1=dtb_s[:, :], scalar2=None, op0=ALU.add))
            SV.do(v.tensor_scalar(out=g4[sl][:, :], in0=aba[sl][:, :],
                                  scalar1=dtb_s[:, :], scalar2=0.0,
                                  op0=ALU.add, op1=ALU.max))
        tok_spv = SV.tok

        @block.scalar
        def _(sc, G=G, tok_spv=tok_spv):
            sc.wait_ge(SS.h, SS.tok)
            sc.wait_ge(SV.h, tok_spv)
            SS.do(sc.activation(out=spx[:, :], in_=ytmp[:, :], func=AF.Abs))
            t0 = SS.tok
            sc.wait_ge(SS.h, t0)
            SS.do(sc.activation(out=spe[:, :], in_=spx[:, :], func=AF.Exp, scale=-1.0))
            t1 = SS.tok
            sc.wait_ge(SS.h, t1)
            SS.do(sc.activation(out=spe[:, :], in_=spe[:, :], func=AF.Ln,
                                bias=MISC[0:4, 2:3], scale=1.0))
        tok_spe = SS.tok

        @block.vector
        def _(v, G=G, sl=sl, tok_spe=tok_spe):
            v.wait_ge(SV.h, SV.tok)
            v.wait_ge(SS.h, tok_spe)
            SV.do(v.tensor_tensor(out=g4[sl][:, :], in0=g4[sl][:, :], in1=spe[:, :],
                                  op=ALU.add))
        tok_ab[G] = SV.tok

    def prep_chunk(ch):
        G, c4 = ch // 4, ch % 4
        sl = G % 2
        pb = ch % 2
        csl = slice(128 * c4, 128 * (c4 + 1))
        c = C[ch] = {}
        cp = C.get(ch - 1, {})

        @block.vector
        def _(v):
            v.wait_ge(SV.h, SV.tok)
            v.wait_ge(SV.h, tok_ab[G])
            v.wait_ge(STN.h, C[ch - 1]["T1"] if ch >= 1 else 0)
            v.wait_ge(sDB1.h, sDB1.n)
            SV.do(v.tensor_scalar(out=gtmp[:, :], in0=g4[sl][:, csl],
                                  scalar1=eA4[:, :], scalar2=-1.0,
                                  op0=ALU.mult, op1=ALU.mult))
            t1 = SV.tok
            v.wait_ge(SV.h, t1)
            SV.do(v.tensor_tensor_scan(out=pkB[:, :], data0=gtmp[:, :],
                                       data1=gtmp[:, :], initial=0.0,
                                       op0=ALU.add, op1=ALU.bypass))
            t2 = SV.tok
            v.wait_ge(SV.h, t2)
            SV.do(v.tensor_scalar(out=pkNB[:, :], in0=pkB[:, :],
                                  scalar1=-1.0, scalar2=None, op0=ALU.mult))
            SV.do(v.tensor_copy(pkBE[:, :], bet4[sl][:, csl]))
        c["V1"] = SV.tok

        @block.sync
        def _(sync):
            sync.wait_ge(SV.h, c["V1"])
            sync.wait_ge(STN.h, C[ch - 1]["T1"] if ch >= 1 else 0)
            for h in range(4):
                sDB1.inc(sync.dma_start(out=brh[h][0:1, :], in_=pkB[h:h + 1, :]), 16)

        @block.tensor
        def _(t):
            if ch == 0:
                t.wait_ge(g["_sem"].h, g["_sem"].n)
            t.wait_ge(SV.h, c["V1"])
            if ch >= 1:
                t.wait_ge(SS.h, C[ch - 1]["S5b"])
            t.wait_ge(SS.h, tok_silu_all[G])
            t.wait_ge(sDB1.h, sDB1.n)
            STN.do(t.transpose(ppk[:, 0:4], pkB[:, :], IDn[0:4, 0:4]))
            STN.do(t.transpose(ppk[:, 4:8], pkNB[:, :], IDn[0:4, 0:4]))
            STN.do(t.transpose(ppk[:, 8:12], pkBE[:, :], IDn[0:4, 0:4]))
            for h in range(4):
                STN.do(t.transpose(ptv[:, 128 * h:128 * (h + 1)],
                                   qkvb[sl][2 + h][:, csl], IDn[:, :]))
            STN.do(t.transpose(ptk[:, 0:128], qkvb[sl][1][:, csl], IDn[:, :]))
            for h in range(4):
                STN.do(t.matmul(pbb[:, 128 * h:128 * (h + 1)],
                                ONE[0:1, 0:128], brh[h][0:1, :],
                                start=True, stop=True))
        c["T1"] = STN.tok

        @block.scalar
        def _(sc):
            sc.wait_ge(SS.h, SS.tok)
            sc.wait_ge(STN.h, c["T1"])
            if ch >= 1:
                sc.wait_ge(SV.h, C[ch - 1]["V4b"])
            sc.wait_ge(sDB2.h, sDB2.n)
            SS.do(sc.activation(out=ksq[:, :], in_=qkvb[sl][1][:, csl], func=AF.Square))
            SS.do(sc.activation(out=qsq[:, :], in_=qkvb[sl][0][:, csl], func=AF.Square))
            SS.do(sc.copy(out=packTs[:, 0:12], in_=ppk[:, 0:12]))
            t1 = SS.tok
            sc.wait_ge(SS.h, t1)
            SS.do(sc.activation(out=eB[:, :], in_=packTs[:, 0:4], func=AF.Exp))
        c["S2"] = SS.tok

        @block.sync
        def _(sync):
            sync.wait_ge(SS.h, c["S2"])
            sync.wait_ge(STN.h, c["T1"])
            sDB2.inc(sync.dma_start(out=bcrow[:, :], in_=packTs[127:128, 0:4]), 16)

        @block.tensor
        def _(t):
            t.wait_ge(SS.h, c["S2"])
            t.wait_ge(sDB2.h, sDB2.n)
            if ch >= 1:
                t.wait_ge(SV.h, C[ch - 1]["V4a"])
            for kh in range(2):
                r0 = 64 * kh
                STN.do(t.matmul(psm[:, kh:kh + 1], ksq[r0:r0 + 64, :],
                                ONE[r0:r0 + 64, 0:1], start=True, stop=True))
                STN.do(t.matmul(psm[:, 2 + kh:3 + kh], qsq[r0:r0 + 64, :],
                                ONE[r0:r0 + 64, 0:1], start=True, stop=True))
            STN.do(t.matmul(pbc[:, 0:4], ONE[0:1, 0:128], bcrow[0:1, 0:4],
                            start=True, stop=True))
        c["T3"] = STN.tok

        @block.vector
        def _(v):
            v.wait_ge(SV.h, SV.tok)
            v.wait_ge(STN.h, c["T3"])
            SV.do(v.tensor_scalar(out=kq4[:, :], in0=psm[:, 0:4], scalar1=1e-24,
                                  scalar2=None, op0=ALU.max))
            SV.do(v.tensor_tensor(out=brv[:, :], in0=pbc[:, 0:4], in1=packTs[:, 0:4],
                                  op=ALU.subtract))
            for h in range(4):
                SV.do(v.tensor_tensor(out=dnat[h][:, :], in0=pbb[:, 128 * h:128 * (h + 1)],
                                      in1=g["pen_nat"][:, :], op=ALU.add))
                SV.do(v.tensor_tensor(out=dtt[h][:, :], in0=pbb[:, 128 * h:128 * (h + 1)],
                                      in1=g["pen_t"][:, :], op=ALU.add))
                SV.do(v.tensor_copy(tvs[pb][h][:, :], ptv[:, 128 * h:128 * (h + 1)]))
        c["V4a"] = SV.tok

        @block.scalar
        def _(sc):
            sc.wait_ge(SS.h, SS.tok)
            sc.wait_ge(SV.h, c["V4a"])
            SS.do(sc.activation(out=sq4[:, :], in_=kq4[:, :], func=AF.Sqrt))
            SS.do(sc.activation(out=wc4[:, :], in_=brv[:, :], func=AF.Exp))
            SS.do(sc.activation(out=eBCb[pb][:, :], in_=pbc[:, 0:4], func=AF.Exp))
            for h in range(4):
                SS.do(sc.activation(out=dnat[h][:, :], in_=dnat[h][:, :], func=AF.Exp,
                                    bias=packTs[:, h:h + 1], scale=-1.0))
                SS.do(sc.activation(out=dtt[h][:, :], in_=dtt[h][:, :], func=AF.Exp,
                                    bias=packTs[:, 4 + h:5 + h], scale=1.0))
        c["S4"] = SS.tok

        @block.vector
        def _(v):
            v.wait_ge(SV.h, SV.tok)
            v.wait_ge(SS.h, c["S4"])
            SV.do(v.reciprocal(out=rc4[:, :], in_=sq4[:, :]))
            tr = SV.tok
            v.wait_ge(SV.h, tr)
            for kh in range(2):
                SV.do(v.tensor_scalar(out=tkpair[:, 64 * kh:64 * (kh + 1)],
                                      in0=ptk[:, 64 * kh:64 * (kh + 1)],
                                      scalar1=rc4[:, kh:kh + 1], scalar2=None,
                                      op0=ALU.mult))
            t2 = SV.tok
            v.wait_ge(SV.h, t2)
            for h, (dst, cof) in enumerate([(tkbA, 0), (tkbB, 0), (tkbA, 64), (tkbB, 64)]):
                kh = h // 2
                SV.do(v.tensor_scalar(out=dst[:, cof:cof + 64],
                                      in0=tkpair[:, 64 * kh:64 * (kh + 1)],
                                      scalar1=packTs[:, 8 + h:9 + h], scalar2=None,
                                      op0=ALU.mult))
                SV.do(v.tensor_scalar(out=tkbar[pb][:, 64 * h:64 * (h + 1)],
                                      in0=tkpair[:, 64 * kh:64 * (kh + 1)],
                                      scalar1=wc4[:, h:h + 1], scalar2=None,
                                      op0=ALU.mult))
            SV.do(v.tensor_tensor(out=nbeb[pb][:, :], in0=packTs[:, 8:12], in1=eB[:, :],
                                  op=ALU.mult))
            t3 = SV.tok
            v.wait_ge(SV.h, t3)
            SV.do(v.tensor_scalar(out=nbeb[pb][:, :], in0=nbeb[pb][:, :], scalar1=-1.0,
                                  scalar2=None, op0=ALU.mult))
            for h in range(4):
                kh = h // 2
                SV.do(v.tensor_scalar(out=s2c[pb][:, h:h + 1], in0=rc4[:, 2 + kh:3 + kh],
                                      scalar1=SCALE, scalar2=None, op0=ALU.mult))
            t4 = SV.tok
            v.wait_ge(SV.h, t4)
            for h in range(4):
                SV.do(v.tensor_scalar(out=rhsb[pb][h][:, :], in0=tvs[pb][h][:, :],
                                      scalar1=packTs[:, 8 + h:9 + h], scalar2=None,
                                      op0=ALU.mult))
            SV.do(v.tensor_tensor(out=s1c[pb][:, :], in0=s2c[pb][:, :], in1=eB[:, :],
                                  op=ALU.mult))
        c["V4b"] = SV.tok

        @block.tensor
        def _(t):
            t.wait_ge(SV.h, c["V4b"])
            STN.do(t.transpose(pkh[:, :], tkpair[:, :], IDn[:, :]))
            STN.do(t.transpose(pkb[0][:, :], tkbA[:, :], IDn[:, :]))
            STN.do(t.transpose(pkb[1][:, :], tkbB[:, :], IDn[:, :]))
        c["T5a"] = STN.tok

        @block.scalar
        def _(sc):
            sc.wait_ge(SS.h, SS.tok)
            sc.wait_ge(STN.h, c["T5a"])
            if ch >= 2:
                sc.wait_ge(STN.h, C[ch - 1]["T5g"])
            SS.do(sc.copy(out=khs[pb][:, :], in_=pkh[:, :]))
            SS.do(sc.copy(out=khb02[pb][:, :], in_=pkb[0][:, :]))
            SS.do(sc.copy(out=khb13[pb][:, :], in_=pkb[1][:, :]))
        c["S5b"] = SS.tok

        @block.tensor
        def _(t):
            t.wait_ge(SS.h, c["S5b"])
            for h in range(4):
                r0 = 64 * (h // 2)
                q = slice(128 * h, 128 * (h + 1))
                khbs = khb02[pb] if h % 2 == 0 else khb13[pb]
                rb = 64 * (h // 2)
                STN.do(t.matmul(pg1[:, q], khbs[rb:rb + 64, :], khs[pb][r0:r0 + 64, :],
                                start=True, stop=True))
                STN.do(t.matmul(pg2[:, q], khs[pb][r0:r0 + 64, :], khbs[rb:rb + 64, :],
                                start=True, stop=True))
                STN.do(t.matmul(pg3[:, q], khs[pb][r0:r0 + 64, :],
                                qkvb[sl][0][r0:r0 + 64, csl], start=True, stop=True))
        c["T5g"] = STN.tok

        @block.vector
        def _(v):
            v.wait_ge(SV.h, SV.tok)
            v.wait_ge(STN.h, c["T5g"])
            for h in range(4):
                q = slice(128 * h, 128 * (h + 1))
                SV.do(v.tensor_tensor(out=anat[h][:, :], in0=dnat[h][:, :],
                                      in1=pg1[:, q], op=ALU.mult))
                SV.do(v.tensor_tensor(out=att_[h][:, :], in0=dtt[h][:, :],
                                      in1=pg2[:, q], op=ALU.mult))
                SV.do(v.tensor_tensor(out=attq[pb][h][:, :], in0=dtt[h][:, :],
                                      in1=pg3[:, q], op=ALU.mult))
            t1 = SV.tok
            v.wait_ge(SV.h, t1)
            for h in range(4):
                SV.do(v.tensor_tensor(out=anat[h][:, :], in0=anat[h][:, :],
                                      in1=g["mask_lsn"][:, :], op=ALU.mult))
                SV.do(v.tensor_tensor(out=att_[h][:, :], in0=att_[h][:, :],
                                      in1=g["mask_usn"][:, :], op=ALU.mult))
            t2 = SV.tok
            v.wait_ge(SV.h, t2)
            for h in range(4):
                SV.do(v.tensor_copy(pbuf[pb][h][:, :], att_[h][:, :]))
        c["V6"] = SV.tok

        tok_Vd = c["V6"]
        cur_n, cur_t = anat, att_
        for lev in range(1, 7):
            @block.tensor
            def _(t, cur_n=cur_n, cur_t=cur_t, lev=lev, tok_Vd=tok_Vd):
                t.wait_ge(SV.h, tok_Vd)
                if lev >= 2:
                    t.wait_ge(SS.h, c[f"Sd{lev - 1}"])
                    t.wait_ge(SV.h, c[f"Vt{lev - 1}"])
                for h in range(4):
                    q = slice(128 * h, 128 * (h + 1))
                    STN.do(t.matmul(pdd[0][:, q], cur_t[h][:, :], cur_n[h][:, :],
                                    start=True, stop=True))
                    STN.do(t.matmul(pdd[1][:, q], cur_n[h][:, :], cur_t[h][:, :],
                                    start=True, stop=True))
            tok_Ta = STN.tok

            new_n = [nnb[h][lev % 2] for h in range(4)]
            new_t = [ntb[h][lev % 2] for h in range(4)]

            @block.scalar
            def _(sc, new_n=new_n, tok_Ta=tok_Ta):
                sc.wait_ge(SS.h, SS.tok)
                sc.wait_ge(STN.h, tok_Ta)
                for h in range(4):
                    q = slice(128 * h, 128 * (h + 1))
                    SS.do(sc.copy(out=new_n[h][:, :], in_=pdd[0][:, q]))
            c[f"Sd{lev}"] = SS.tok

            @block.vector
            def _(v, new_t=new_t, tok_Ta=tok_Ta):
                v.wait_ge(SV.h, SV.tok)
                v.wait_ge(STN.h, tok_Ta)
                for h in range(4):
                    q = slice(128 * h, 128 * (h + 1))
                    SV.do(v.tensor_copy(new_t[h][:, :], pdd[1][:, q]))
            c[f"Vt{lev}"] = SV.tok

            @block.tensor
            def _(t, new_n=new_n, new_t=new_t, lev=lev):
                t.wait_ge(SS.h, c[f"Sd{lev}"])
                t.wait_ge(SV.h, c[f"Vt{lev}"])
                for h in range(4):
                    q = slice(128 * h, 128 * (h + 1))
                    STN.do(t.matmul(ppr[:, q], new_n[h][:, :], pbuf[pb][h][:, :],
                                    start=True, stop=False))
                    STN.do(t.matmul(ppr[:, q], IDn[:, :], new_t[h][:, :],
                                    start=False, stop=True))
            tok_Tb = STN.tok

            @block.vector
            def _(v, tok_Tb=tok_Tb):
                v.wait_ge(SV.h, SV.tok)
                v.wait_ge(STN.h, tok_Tb)
                for h in range(4):
                    q = slice(128 * h, 128 * (h + 1))
                    SV.do(v.tensor_tensor(out=pbuf[pb][h][:, :], in0=pbuf[pb][h][:, :],
                                          in1=ppr[:, q], op=ALU.add))
            tok_Vd = SV.tok
            cur_n, cur_t = new_n, new_t
        c["Vd"] = tok_Vd

    def main_chunk(ch):
        G, c4 = ch // 4, ch % 4
        sl = G % 2
        pb = ch % 2
        csl = slice(128 * c4, 128 * (c4 + 1))
        c = C[ch]

        @block.tensor
        def _(t):
            t.wait_ge(SV.h, C[ch - 1]["V12s"] if ch >= 1 else 0)
            t.wait_ge(SS.h, C[ch - 1]["S14"] if ch >= 1 else 0)
            t.wait_ge(SS.h, c["S5b"])
            for hh in ([0, 1], [2, 3]):
                for h in hh:
                    r0 = 64 * (h // 2)
                    q = slice(128 * h, 128 * (h + 1))
                    STN.do(t.matmul(pks[:, q], khs[pb][r0:r0 + 64, :],
                                    SAB[h % 2][r0:r0 + 64, :], start=True, stop=True))
                for h in hh:
                    r0 = 64 * (h // 2)
                    q = slice(128 * h, 128 * (h + 1))
                    STN.do(t.matmul(po1[:, q], qkvb[sl][0][r0:r0 + 64, csl],
                                    SAB[h % 2][r0:r0 + 64, :], start=True, stop=True))
            tok_qdone[ch] = STN.tok
        c["T5m"] = STN.tok

        @block.vector
        def _(v):
            v.wait_ge(SV.h, SV.tok)
            v.wait_ge(STN.h, c["T5m"])
            for h in range(4):
                q = slice(128 * h, 128 * (h + 1))
                SV.do(v.scalar_tensor_tensor(out=rhsb[pb][h][:, :], in0=pks[:, q],
                                             scalar=nbeb[pb][:, h:h + 1],
                                             in1=rhsb[pb][h][:, :],
                                             op0=ALU.mult, op1=ALU.add))
        c["V8f"] = SV.tok

        @block.tensor
        def _(t):
            t.wait_ge(SV.h, c["V8f"])
            t.wait_ge(SV.h, c["Vd"])
            for h in range(4):
                q = slice(128 * h, 128 * (h + 1))
                STN.do(t.matmul(pdl[:, q], pbuf[pb][h][:, :], rhsb[pb][h][:, :],
                                start=True, stop=True))
        c["T9"] = STN.tok

        @block.vector
        def _(v):
            v.wait_ge(SV.h, SV.tok)
            v.wait_ge(STN.h, c["T9"])
            for h in range(4):
                q = slice(128 * h, 128 * (h + 1))
                SV.do(v.tensor_tensor(out=delt[h][:, :], in0=rhsb[pb][h][:, :],
                                      in1=pdl[:, q], op=ALU.add))
        c["V10"] = SV.tok

        @block.tensor
        def _(t):
            t.wait_ge(SV.h, c["V10"])
            for h in range(4):
                q = slice(128 * h, 128 * (h + 1))
                STN.do(t.matmul(po2[:, q], attq[pb][h][:, :], delt[h][:, :],
                                start=True, stop=True))
        c["T11"] = STN.tok

        @block.vector
        def _(v):
            v.wait_ge(SV.h, SV.tok)
            v.wait_ge(STN.h, c["T11"])
            for h in range(4):
                q = slice(128 * h, 128 * (h + 1))
                SV.do(v.tensor_scalar(out=obuf[h][:, :], in0=po1[:, q],
                                      scalar1=s1c[pb][:, h:h + 1], scalar2=None,
                                      op0=ALU.mult))
            t1 = SV.tok
            v.wait_ge(SV.h, t1)
            for h in range(4):
                q = slice(128 * h, 128 * (h + 1))
                SV.do(v.scalar_tensor_tensor(out=obuf[h][:, :], in0=po2[:, q],
                                             scalar=s2c[pb][:, h:h + 1],
                                             in1=obuf[h][:, :],
                                             op0=ALU.mult, op1=ALU.add))
        c["V12a"] = SV.tok

        @block.tensor
        def _(t):
            t.wait_ge(SV.h, c["V12a"])
            for h in range(4):
                r0 = 64 * (h // 2)
                STN.do(t.matmul(puu[h % 2][r0:r0 + 64, :],
                                tkbar[pb][:, 64 * h:64 * (h + 1)],
                                delt[h][:, :], start=True, stop=True,
                                tile_position=(0, r0)))
        c["T11b"] = STN.tok

        @block.scalar
        def _(sc):
            sc.wait_ge(SS.h, SS.tok)
            sc.wait_ge(SV.h, c["V12a"])
            toks = {}
            for h in range(4):
                if h >= 2:
                    sc.wait_ge(SS.h, toks[h - 2])
                toks[h] = SS.do(sc.activation(out=osq[h % 2][:, :], in_=obuf[h][:, :],
                                              func=AF.Square,
                                              accum_out=rmsc[:, h:h + 1]))
            t1 = SS.tok
            sc.wait_ge(SS.h, t1)
            SS.do(sc.activation(out=rq4[:, :], in_=rmsc[:, :], func=AF.Sqrt,
                                bias=MISC[:, 1:2], scale=1.0 / DV))
        c["S12"] = SS.tok

        @block.vector
        def _(v):
            v.wait_ge(SV.h, SV.tok)
            v.wait_ge(STN.h, c["T11b"])
            for h in range(4):
                r0 = 64 * (h // 2)
                SX = SAB[h % 2]
                SV.do(v.scalar_tensor_tensor(out=SX[r0:r0 + 64, :], in0=SX[r0:r0 + 64, :],
                                             scalar=eBCb[pb][r0:r0 + 64, h:h + 1],
                                             in1=puu[h % 2][r0:r0 + 64, :],
                                             op0=ALU.mult, op1=ALU.add))
        c["V12s"] = SV.tok

        @block.vector
        def _(v):
            v.wait_ge(SV.h, SV.tok)
            v.wait_ge(SS.h, c["S12"])
            SV.do(v.reciprocal(out=rr4[:, :], in_=rq4[:, :]))
            t1 = SV.tok
            v.wait_ge(SV.h, t1)
            for h in range(4):
                SV.do(v.tensor_scalar(out=obuf[h][:, :], in0=obuf[h][:, :],
                                      scalar1=rr4[:, h:h + 1], scalar2=None,
                                      op0=ALU.mult))
        c["V12b"] = SV.tok

        @block.tensor
        def _(t):
            t.wait_ge(SV.h, c["V12b"])
            for h in range(4):
                q = slice(128 * h, 128 * (h + 1))
                STN.do(t.transpose(pot[:, q], obuf[h][:, :], IDn[:, :]))
        c["T13"] = STN.tok

        osl = (ch % 2) * 4
        @block.scalar
        def _(sc, osl=osl):
            sc.wait_ge(SS.h, SS.tok)
            sc.wait_ge(STN.h, c["T13"])
            j = ch // 8
            cof = (ch % 8) * 128
            toks = {}
            for h in range(4):
                q = slice(128 * h, 128 * (h + 1))
                sc.wait_ge(sO[osl + h].h, sO[osl + h].n)
                toks[h] = SS.do(sc.copy(out=oship[osl + h][:, :], in_=pot[:, q]))
            for h in range(4):
                sc.wait_ge(SS.h, toks[h])
                sO[osl + h].inc(sc.dma_start(out=cc2in[j, 128 * h:128 * (h + 1), cof:cof + 128],
                                             in_=oship[osl + h][:, :]), 16)
        c["S14"] = SS.tok

    NCHN = 4 * NG
    for ch in range(NCHN):
        if ch % 4 == 0:
            group_sections(ch // 4)
        prep_chunk(ch)
        if wdprep is not None:
            wdprep(block, ch)
        if ch >= 1:
            main_chunk(ch - 1)
    main_chunk(NCHN - 1)

    @block.gpsimd
    def _(gp):
        for h in range(8):
            gp.wait_ge(sO[h].h, sO[h].n)
        gp.wait_ge(SV.h, SV.tok)
        gp.wait_ge(SS.h, SS.tok)
        gp.wait_ge(STN.h, STN.tok)

    return dict(sO=sO)


P5_MAX = 99
C1 = float(1.0 / np.sqrt(2048.0))
C2 = float(1.0 / np.sqrt(8192.0))


def tile_lhsT(W, nit):
    nin = W.shape[1]
    A = W.reshape(nit, 128, nin // 128, 128)
    A = np.ascontiguousarray(A.transpose(0, 3, 2, 1))
    return A.reshape(nit, 128, nin)


def emit_phase5(nc, block, ex, g, cc2out, z_scr, xn_scr, x_in, y_out,
                WO2, WU2, WG2, WD2, vec, entry=None):
    NTC = TPC // 256
    buf1 = [sbuf(nc, ex, f"p5b1_{f}", [128, 256]) for f in range(16)]
    xhB = [sbuf(nc, ex, f"p5xb{f}", [128, 256]) for f in range(16)]
    ub = [sbuf(nc, ex, f"p5ub{i}", [128, 256]) for i in range(64)]
    sres = [sbuf(nc, ex, f"p5sr{f}", [128, 256]) for f in range(16)]
    wsl = [sbuf(nc, ex, f"p5w{i}", [128, 2048]) for i in range(4)]
    zsl = [sbuf(nc, ex, f"p5z{i}", [128, 256]) for i in range(2)]
    sgt = [sbuf(nc, ex, f"p5sg{i}", [128, 256]) for i in range(2)]
    ut = [sbuf(nc, ex, f"p5ut{i}", [128, 256]) for i in range(2)]
    btmp = [sbuf(nc, ex, f"p5bt{i}", [128, 256]) for i in range(4)]
    outst = [sbuf(nc, ex, f"p5o{i}", [128, 2048]) for i in range(2)]
    xld = [sbuf(nc, ex, f"p5x{i}", [128, 2048]) for i in range(2)]
    sut = sbuf(nc, ex, "p5su", [128, 16])
    sudpt = sbuf(nc, ex, "p5sd", [128, 64])
    wsut = sbuf(nc, ex, "p5wu", [128, 64])
    wsgt = sbuf(nc, ex, "p5wg", [128, 64])
    wsdt = sbuf(nc, ex, "p5wd", [128, 16])
    nwt = sbuf(nc, ex, "p5nw", [128, 1])
    wsuC2 = sbuf(nc, ex, "p5wc2", [128, 64])
    wsgC = sbuf(nc, ex, "p5wgc", [128, 64])
    wsdC = sbuf(nc, ex, "p5wdc", [128, 16])
    PB = [psum(nc, ex, f"p5P{i}", [128, 512]) for i in range(8)]
    pAb = [PB[0], PB[1]]
    pBb = [PB[2], PB[3]]
    pUb = [PB[0], PB[1]]
    pGb = [PB[2], PB[3]]
    pFb = [PB[4], PB[5]]
    pDb = [PB[4], PB[5]]
    pT = PB[6]

    SV = ES(nc, ex, "p5SV")
    SS = ES(nc, ex, "p5SS")
    STN = ES(nc, ex, "p5ST")
    sW = [Sem(nc, ex, f"p5sw{i}") for i in range(4)]
    sOZ = [Sem(nc, ex, f"p5oz{i}") for i in range(2)]
    sGD = Sem(nc, ex, "p5gd")
    sXN = Sem(nc, ex, "p5xn")
    sXL = [Sem(nc, ex, f"p5xl{i}") for i in range(2)]
    sST = [Sem(nc, ex, f"p5st{i}") for i in range(2)]
    sPRE = Sem(nc, ex, "p5pre")

    IDn = g["ident"]

    @block.sync
    def _(sync):
        if entry is not None:
            sync.wait_ge(entry[0], entry[1])
        for src_ap, dst in [(vec["su_t"], sut), (vec["sudp_t"], sudpt),
                            (vec["wsu_t"], wsut), (vec["wsg_t"], wsgt),
                            (vec["wsd_t"], wsdt), (vec["nw"], nwt)]:
            sPRE.inc(sync.dma_start(out=dst[:, :], in_=src_ap[:, :]), 16)

    @block.vector
    def _(v):
        v.wait_ge(sPRE.h, sPRE.n)
        SV.do(v.tensor_scalar(out=wsuC2[:, :], in0=wsut[:, :], scalar1=C1,
                              scalar2=None, op0=ALU.mult))
        t1 = SV.tok
        v.wait_ge(SV.h, t1)
        SV.do(v.tensor_tensor(out=wsuC2[:, :], in0=wsuC2[:, :], in1=sudpt[:, :],
                              op=ALU.mult))
        SV.do(v.tensor_scalar(out=wsgC[:, :], in0=wsgt[:, :], scalar1=C1,
                              scalar2=None, op0=ALU.mult))
        SV.do(v.tensor_scalar(out=wsdC[:, :], in0=wsdt[:, :], scalar1=C2,
                              scalar2=None, op0=ALU.mult))
    tok_vprep = SV.tok

    tok_chunk_T = {-1: 0}
    tok_chunk_V = {-1: 0}
    tok_chunk_S = {-1: 0}
    st_w = [0, 0, 0, 0]
    tok_outev = [0, 0]

    def emit_chunk(b, tc):
        ci = b * NTC + tc
        csl = slice(256 * tc, 256 * (tc + 1))

        @block.sync
        def _(sync):
            sync.wait_ge(STN.h, tok_chunk_T[ci - 1])
            sync.wait_ge(SV.h, tok_chunk_V[ci - 1])
            for f in range(16):
                sGD.inc(sync.dma_start(
                    out=buf1[f][:, :],
                    in_=cc2out[4 * b + f // 4, 128 * (f % 4):128 * (f % 4 + 1), csl]), 16)
        tok_oload = sGD.n

        if P5_MAX < 1:
            tok_chunk_T[ci] = STN.tok; tok_chunk_V[ci] = SV.tok; tok_chunk_S[ci] = SS.tok; return
        tok_zg = {}
        for f in range(16):
            @block.sync
            def _(sync, f=f):
                if f >= 2:
                    sync.wait_ge(SV.h, tok_zg[f - 2])
                else:
                    sync.wait_ge(SV.h, tok_chunk_V[ci - 1])
                sOZ[f % 2].inc(sync.dma_start(out=zsl[f % 2][:, :],
                                              in_=z_scr[b, f, :, csl]), 16)

            @block.scalar
            def _(sc, f=f):
                sc.wait_ge(SS.h, SS.tok)
                sc.wait_ge(sOZ[f % 2].h, sOZ[f % 2].n)
                if f >= 2:
                    sc.wait_ge(SV.h, tok_zg[f - 2])
                else:
                    sc.wait_ge(SV.h, tok_chunk_V[ci - 1])
                if f == 0 and ci == 0:
                    sc.wait_ge(sPRE.h, sPRE.n)
                SS.do(sc.activation(out=sgt[f % 2][:, :], in_=zsl[f % 2][:, :],
                                    func=AF.Silu))
            tok_sz = SS.tok

            @block.vector
            def _(v, f=f, tok_sz=tok_sz):
                v.wait_ge(SV.h, SV.tok)
                v.wait_ge(SS.h, tok_sz)
                if f == 0:
                    v.wait_ge(sGD.h, tok_oload)
                    v.wait_ge(SV.h, tok_vprep)
                SV.do(v.scalar_tensor_tensor(out=buf1[f][:, :], in0=buf1[f][:, :],
                                             scalar=nwt[:, :], in1=sgt[f % 2][:, :],
                                             op0=ALU.mult, op1=ALU.mult))
            tok_zg[f] = SV.tok
        tok_gated = SV.tok

        if P5_MAX < 2:
            tok_chunk_T[ci] = STN.tok; tok_chunk_V[ci] = SV.tok; tok_chunk_S[ci] = SS.tok; return
        tok_opev = {}
        for j in range(16):
            slot = j % 2

            @block.sync
            def _(sync, j=j, slot=slot):
                sync.wait_ge(STN.h, st_w[slot])
                sW[slot].inc(sync.dma_start(out=wsl[slot][:, :], in_=WO2[j, :, :]), 16)

            @block.tensor
            def _(t, j=j, slot=slot):
                if j == 0:
                    t.wait_ge(SV.h, tok_gated)
                    t.wait_ge(SS.h, tok_chunk_S[ci - 1])
                    if ci == 0:
                        t.wait_ge(g["_sem"].h, g["_sem"].n)
                if j >= 2:
                    t.wait_ge(SS.h, tok_opev[j - 2])
                t.wait_ge(sW[slot].h, sW[slot].n)
                mm = None
                for k in range(16):
                    mm = t.matmul(pAb[j % 2][:, 0:256],
                                  wsl[slot][:, k * 128:(k + 1) * 128],
                                  buf1[k][:, :], start=(k == 0), stop=(k == 15))
                STN.do(mm)
                st_w[slot] = STN.tok

            @block.scalar
            def _(sc, j=j):
                sc.wait_ge(SS.h, SS.tok)
                sc.wait_ge(STN.h, STN.tok)
                SS.do(sc.copy(out=sres[j][:, :], in_=pAb[j % 2][:, 0:256]))
            tok_opev[j] = SS.tok
        tok_opT = STN.tok

        if P5_MAX < 3:
            tok_chunk_T[ci] = STN.tok; tok_chunk_V[ci] = SV.tok; tok_chunk_S[ci] = SS.tok; return
        @block.sync
        def _(sync):
            sync.wait_ge(STN.h, tok_opT)
            for f in range(16):
                sXN.inc(sync.dma_start(out=buf1[f][:, :], in_=xn_scr[b, f, :, csl]), 16)
        tok_xnl = sXN.n

        @block.vector
        def _(v):
            v.wait_ge(SV.h, SV.tok)
            v.wait_ge(sXN.h, tok_xnl)
            for f in range(16):
                SV.do(v.tensor_scalar(out=buf1[f][:, :], in0=buf1[f][:, :],
                                      scalar1=sut[:, f:f + 1], scalar2=None,
                                      op0=ALU.mult))
        tok_sumul = SV.tok

        tok_h1ev = {}
        for f in range(16):
            @block.tensor
            def _(t, f=f):
                if f == 0:
                    t.wait_ge(SV.h, tok_sumul)
                if f >= 2:
                    t.wait_ge(SV.h, tok_h1ev[f - 2])
                STN.do(t.matmul(pBb[f % 2][:, 0:256],
                                g["h128"][:, :], buf1[f][:, :], start=True, stop=True))
            tok_h1mm = STN.tok

            @block.vector
            def _(v, f=f, tok_h1mm=tok_h1mm):
                v.wait_ge(SV.h, SV.tok)
                v.wait_ge(STN.h, tok_h1mm)
                SV.do(v.tensor_copy(xhB[f][:, :], pBb[f % 2][:, 0:256]))
            tok_h1ev[f] = SV.tok
        tok_h1mm_last = STN.tok

        @block.vector
        def _(v):
            v.wait_ge(SV.h, SV.tok)
            v.wait_ge(STN.h, tok_h1mm_last)
            cur, nxt = xhB, buf1
            for hh in (1, 2, 4, 8):
                for i in range(16):
                    prt = i ^ hh
                    if i & hh:
                        SV.do(v.tensor_tensor(out=nxt[i][:, :], in0=cur[prt][:, :],
                                              in1=cur[i][:, :], op=ALU.subtract))
                    else:
                        SV.do(v.tensor_tensor(out=nxt[i][:, :], in0=cur[i][:, :],
                                              in1=cur[prt][:, :], op=ALU.add))
                tb = SV.tok
                v.wait_ge(SV.h, tb)
                cur, nxt = nxt, cur
        tok_xh = SV.tok

        if P5_MAX < 4:
            tok_chunk_T[ci] = STN.tok; tok_chunk_V[ci] = SV.tok; tok_chunk_S[ci] = SS.tok; return
        tok_sg = {}
        tok_ue = {}
        tok_stt = {}
        tok_f2 = {}
        for it in range(65):
            if it < 64:
                @block.sync
                def _(sync, it=it):
                    sync.wait_ge(STN.h, st_w[it % 2])
                    sW[it % 2].inc(sync.dma_start(out=wsl[it % 2][:, :],
                                                  in_=WU2[it, :, :]), 16)
                    sync.wait_ge(STN.h, st_w[2 + it % 2])
                    sW[2 + it % 2].inc(sync.dma_start(out=wsl[2 + it % 2][:, :],
                                                      in_=WG2[it, :, :]), 16)

            @block.tensor
            def _(t, it=it):
                if it >= 1:
                    t.wait_ge(SV.h, tok_stt[it - 1])
                    if it >= 3:
                        t.wait_ge(SV.h, tok_ue[it - 3])
                    STN.do(t.matmul(pFb[(it - 1) % 2][:, 0:256], g["h128"][:, :],
                                    ut[(it - 1) % 2][:, :], start=True, stop=True))
                    tok_f2[it - 1] = STN.tok
                if it < 64:
                    if it == 0:
                        t.wait_ge(SV.h, tok_xh)
                    if it < 2:
                        t.wait_ge(SS.h, tok_opev[15])
                    if it >= 2:
                        t.wait_ge(SS.h, tok_sg[it - 2])
                        t.wait_ge(SV.h, tok_stt[it - 2])
                    t.wait_ge(sW[it % 2].h, sW[it % 2].n)
                    t.wait_ge(sW[2 + it % 2].h, sW[2 + it % 2].n)
                    mm = None
                    for k in range(16):
                        mm = t.matmul(pUb[it % 2][:, 0:256], wsl[it % 2][:, k * 128:(k + 1) * 128],
                                      xhB[k][:, :], start=(k == 0), stop=(k == 15))
                    STN.do(mm)
                    mm = None
                    for k in range(16):
                        mm = t.matmul(pGb[it % 2][:, 0:256], wsl[2 + it % 2][:, k * 128:(k + 1) * 128],
                                      xhB[k][:, :], start=(k == 0), stop=(k == 15))
                    STN.do(mm)
                    st_w[it % 2] = STN.tok
                    st_w[2 + it % 2] = STN.tok
            tok_ugT = STN.tok

            if it < 64:
                @block.scalar
                def _(sc, it=it, tok_ugT=tok_ugT):
                    sc.wait_ge(SS.h, SS.tok)
                    sc.wait_ge(STN.h, tok_ugT)
                    SS.do(sc.activation(out=sgt[it % 2][:, :], in_=pGb[it % 2][:, 0:256],
                                        func=AF.Silu, scale=wsgC[:, it:it + 1]))
                tok_sg[it] = SS.tok

                @block.vector
                def _(v, it=it):
                    v.wait_ge(SV.h, SV.tok)
                    v.wait_ge(SS.h, tok_sg[it])
                    if it >= 2:
                        v.wait_ge(STN.h, tok_f2[it - 2])
                    SV.do(v.scalar_tensor_tensor(out=ut[it % 2][:, :],
                                                 in0=pUb[it % 2][:, 0:256],
                                                 scalar=wsuC2[:, it:it + 1],
                                                 in1=sgt[it % 2][:, :],
                                                 op0=ALU.mult, op1=ALU.mult))
                tok_stt[it] = SV.tok

            if it >= 1:
                @block.vector
                def _(v, it=it):
                    v.wait_ge(SV.h, SV.tok)
                    v.wait_ge(STN.h, tok_f2[it - 1])
                    SV.do(v.tensor_copy(ub[it - 1][:, :], pFb[(it - 1) % 2][:, 0:256]))
                tok_ue[it - 1] = SV.tok

        if P5_MAX < 5:
            tok_chunk_T[ci] = STN.tok; tok_chunk_V[ci] = SV.tok; tok_chunk_S[ci] = SS.tok; return
        tok_ubf = tok_ue[63]

        if P5_MAX < 6:
            tok_chunk_T[ci] = STN.tok; tok_chunk_V[ci] = SV.tok; tok_chunk_S[ci] = SS.tok; return
        tok_dsil = {}
        tok_dprod = {}
        for j in range(16):
            @block.sync
            def _(sync, j=j):
                for q4 in range(4):
                    sync.wait_ge(STN.h, st_w[q4])
                    sW[q4].inc(sync.dma_start(out=wsl[q4][:, :],
                                              in_=WD2[j, :, 2048 * q4:2048 * (q4 + 1)]), 16)

            @block.tensor
            def _(t, j=j):
                if j == 0:
                    t.wait_ge(SV.h, tok_ubf)
                if j >= 2:
                    t.wait_ge(SS.h, tok_dsil[j - 2])
                mm = None
                for q4 in range(4):
                    t.wait_ge(sW[q4].h, sW[q4].n)
                    for kk in range(16):
                        k = 16 * q4 + kk
                        mm = t.matmul(pDb[j % 2][:, 0:256], wsl[q4][:, kk * 128:(kk + 1) * 128],
                                      ub[k][:, :], start=(k == 0), stop=(k == 63))
                STN.do(mm)
                for q4 in range(4):
                    st_w[q4] = STN.tok
            tok_dT = STN.tok

            @block.scalar
            def _(sc, j=j, tok_dT=tok_dT):
                sc.wait_ge(SS.h, SS.tok)
                sc.wait_ge(STN.h, tok_dT)
                if j >= 2:
                    sc.wait_ge(SV.h, tok_dprod[j - 2])
                SS.do(sc.activation(out=sgt[j % 2][:, :], in_=pDb[j % 2][:, 0:256],
                                    func=AF.Silu, scale=wsdC[:, j:j + 1]))
            tok_dsil[j] = SS.tok

            @block.vector
            def _(v, j=j):
                v.wait_ge(SV.h, SV.tok)
                v.wait_ge(SS.h, tok_dsil[j])
                SV.do(v.tensor_tensor(out=sres[j][:, :], in0=sres[j][:, :],
                                      in1=sgt[j % 2][:, :], op=ALU.mult))
            tok_dprod[j] = SV.tok
        tok_prod = SV.tok

        if P5_MAX < 7:
            tok_chunk_T[ci] = STN.tok; tok_chunk_V[ci] = SV.tok; tok_chunk_S[ci] = SS.tok; return
        @block.sync
        def _(sync):
            for th in range(2):
                sync.wait_ge(SV.h, tok_outev[th])
                sXL[th].inc(sync.dma_start(
                    out=xld[th][:, :],
                    in_=x_in[b, 256 * tc + 128 * th:256 * tc + 128 * th + 128, :]), 16)

        for th in range(2):
            tok_twave = {}
            tok_vwave = {}
            for wv in range(4):
                @block.tensor
                def _(t, th=th, wv=wv):
                    if wv >= 1:
                        t.wait_ge(SV.h, tok_vwave[wv - 1])
                    elif th == 1:
                        t.wait_ge(SV.h, tok_outev[0])
                    if th == 0 and wv == 0:
                        t.wait_ge(SV.h, tok_prod)
                        t.wait_ge(SV.h, tok_outev[0])
                        t.wait_ge(SV.h, tok_outev[1])
                    for j4 in range(4):
                        j = 4 * wv + j4
                        STN.do(t.transpose(pT[:, 128 * j4:128 * (j4 + 1)],
                                           sres[j][:, 128 * th:128 * th + 128],
                                           IDn[:, :]))
                tok_twave[wv] = STN.tok

                @block.vector
                def _(v, th=th, wv=wv, tok_twave=tok_twave):
                    v.wait_ge(SV.h, SV.tok)
                    v.wait_ge(STN.h, tok_twave[wv])
                    if wv == 0:
                        v.wait_ge(sXL[th].h, sXL[th].n)
                        v.wait_ge(sST[th].h, sST[th].n)
                    for j4 in range(4):
                        j = 4 * wv + j4
                        SV.do(v.tensor_tensor(
                            out=outst[th][:, 128 * j:128 * (j + 1)],
                            in0=pT[:, 128 * j4:128 * (j4 + 1)],
                            in1=xld[th][:, 128 * j:128 * (j + 1)], op=ALU.add))
                tok_vwave[wv] = SV.tok
            tok_outev[th] = SV.tok

            @block.gpsimd
            def _(gp, th=th):
                gp.wait_ge(SV.h, tok_outev[th])
                sST[th].inc(gp.dma_start(
                    out=y_out[b, 256 * tc + 128 * th:256 * tc + 128 * th + 128, :],
                    in_=outst[th][:, :]), 16)

        tok_chunk_T[ci] = STN.tok
        tok_chunk_V[ci] = SV.tok
        tok_chunk_S[ci] = SS.tok

    for b in range(B):
        for tc in range(NTC):
            emit_chunk(b, tc)

    @block.gpsimd
    def _(gp):
        for th in range(2):
            gp.wait_ge(sST[th].h, sST[th].n)

    return dict(sST=sST)

def _build_p1():
    nc = bass.Bass("TRN2", target_bir_lowering=False)
    x_in = nc.dram_tensor("x_in", [B, TPC, D], F32, kind="ExternalInput").ap()
    wT = nc.dram_tensor("wT", [D, IPD], F32, kind="ExternalInput").ap()
    cc1in = nc.dram_tensor("cc1in", [8, SH_CH, TPC], F32, kind="ExternalOutput").ap()
    z_scr = nc.dram_tensor("z_scr", [B, 16, 128, TPC], F32, kind="ExternalOutput").ap()
    xn_scr = nc.dram_tensor("xn_scr", [B, 16, 128, TPC], F32, kind="ExternalOutput").ap()
    with ExitStack() as ex:
        block = ex.enter_context(nc.Block())
        g = load_consts(nc, block, ex, ["ident", "miscc"])
        emit_phase1(nc, block, ex, g, cc1in, z_scr, xn_scr, x_in, wT)
    return nc


def _build_p3():
    nc = bass.Bass("TRN2", target_bir_lowering=False)
    cc1out = nc.dram_tensor("cc1out", [8, SH_CH, TPC], F32, kind="ExternalInput").ap()
    alog4 = nc.dram_tensor("alog4", [4, 1], F32, kind="ExternalInput").ap()
    dtb4 = nc.dram_tensor("dtb4", [4, 1], F32, kind="ExternalInput").ap()
    cwt = nc.dram_tensor("cwt", [128, 24], F32, kind="ExternalInput").ap()
    cc2in = nc.dram_tensor("cc2in", [8, 512, TPC], F32, kind="ExternalOutput").ap()
    with ExitStack() as ex:
        block = ex.enter_context(nc.Block())
        g = load_consts(nc, block, ex, [n for n, _ in CONST_SHAPES])
        emit_phase3(nc, block, ex, g, cc1out, cc2in, alog4, dtb4, cwt)
    return nc


def _build_p5():
    nc = bass.Bass("TRN2", target_bir_lowering=False)
    cc2out = nc.dram_tensor("cc2out", [8, 512, TPC], F32, kind="ExternalInput").ap()
    z_scr = nc.dram_tensor("z_scr", [B, 16, 128, TPC], F32, kind="ExternalInput").ap()
    xn_scr = nc.dram_tensor("xn_scr", [B, 16, 128, TPC], F32, kind="ExternalInput").ap()
    x_in = nc.dram_tensor("x_in", [B, TPC, D], F32, kind="ExternalInput").ap()
    WO2 = nc.dram_tensor("WO2", [16, 128, 2048], F32, kind="ExternalInput").ap()
    WU2 = nc.dram_tensor("WU2", [64, 128, 2048], F32, kind="ExternalInput").ap()
    WG2 = nc.dram_tensor("WG2", [64, 128, 2048], F32, kind="ExternalInput").ap()
    WD2 = nc.dram_tensor("WD2", [16, 128, 8192], F32, kind="ExternalInput").ap()
    vec = {}
    for n, shp in [("su_t", [128, 16]), ("sudp_t", [128, 64]), ("wsu_t", [128, 64]),
                   ("wsg_t", [128, 64]), ("wsd_t", [128, 16]), ("nw", [128, 1])]:
        vec[n] = nc.dram_tensor(n, shp, F32, kind="ExternalInput").ap()
    y_out = nc.dram_tensor("y_out", [B, TPC, D], F32, kind="ExternalOutput").ap()
    with ExitStack() as ex:
        block = ex.enter_context(nc.Block())
        g = load_consts(nc, block, ex, ["ident", "h128", "miscc"])
        emit_phase5(nc, block, ex, g, cc2out, z_scr, xn_scr, x_in, y_out,
                    WO2, WU2, WG2, WD2, vec)
    return nc


def kernel_unfused(**inputs):
    consts = host_constants()
    x = np.ascontiguousarray(inputs["L0_attn_x"], dtype=np.float32)
    wT = np.ascontiguousarray(inputs["L0_attn_in_proj_w"].T.astype(np.float32))
    convw = np.asarray(inputs["L0_attn_conv_w"], np.float32)
    alog = np.asarray(inputs["L0_attn_A_log"], np.float32)
    dtb = np.asarray(inputs["L0_attn_dt_bias"], np.float32)
    nw = np.asarray(inputs["L0_attn_norm_weight"], np.float32)
    WO2 = tile_lhsT(np.ascontiguousarray(inputs["L0_attn_out_proj_w"], dtype=np.float32), 16)
    WU2 = tile_lhsT(np.ascontiguousarray(inputs["L0_ffn_W_up"], dtype=np.float32), 64)
    WG2 = tile_lhsT(np.ascontiguousarray(inputs["L0_ffn_W_gate"], dtype=np.float32), 64)
    WD2 = tile_lhsT(np.ascontiguousarray(inputs["L0_ffn_W_down"], dtype=np.float32), 16)
    su = np.asarray(inputs["L0_ffn_SU_ug"], np.float32)
    sdp = np.asarray(inputs["L0_ffn_SU_dp"], np.float32)
    wsug = np.asarray(inputs["L0_ffn_Wscale_ug"], np.float32)
    wsdp = np.asarray(inputs["L0_ffn_Wscale_dp"], np.float32)
    vecs = dict(su_t=np.ascontiguousarray(su.reshape(16, 128).T),
                sudp_t=np.ascontiguousarray(sdp.reshape(64, 128).T),
                wsu_t=np.ascontiguousarray(wsug[:INTER].reshape(64, 128).T),
                wsg_t=np.ascontiguousarray(wsug[INTER:].reshape(64, 128).T),
                wsd_t=np.ascontiguousarray(wsdp.reshape(16, 128).T),
                nw=np.ascontiguousarray(nw.reshape(128, 1)))
    cdict = {("c_" + n): consts[n] for n, _ in CONST_SHAPES}

    nc1 = _build_p1()
    im1 = []
    for c in range(NCORES):
        d = {"x_in": np.ascontiguousarray(x[:, c * TPC:(c + 1) * TPC, :]), "wT": wT}
        d.update({k: cdict[k] for k in ("c_ident", "c_miscc")})
        im1.append(d)
    r1 = run_bass_kernel_spmd(nc1, im1, core_ids=list(range(NCORES))).results

    cc1out = [np.empty((8, SH_CH, TPC), np.float32) for _ in range(NCORES)]
    for j in range(NCORES):
        for i in range(NCORES):
            cc1out[j][i] = r1[i]["cc1in"].reshape(8, SH_CH, TPC)[j]

    nc3 = _build_p3()
    im3 = []
    for c in range(NCORES):
        u = c % 4
        cwt = np.zeros((128, 24), np.float32)
        rows = np.concatenate([convw[128 * u:128 * (u + 1)],
                               convw[512 + 128 * u:512 + 128 * (u + 1)],
                               convw[1024 + 512 * u:1024 + 512 * (u + 1)]], axis=0)
        for t in range(6):
            cwt[:, 4 * t:4 * t + 4] = rows[128 * t:128 * (t + 1)]
        d = {"cc1out": cc1out[c],
             "alog4": np.ascontiguousarray(alog[4 * u:4 * u + 4].reshape(4, 1)),
             "dtb4": np.ascontiguousarray(dtb[4 * u:4 * u + 4].reshape(4, 1)),
             "cwt": cwt}
        d.update(cdict)
        im3.append(d)
    r3 = run_bass_kernel_spmd(nc3, im3, core_ids=list(range(NCORES))).results

    cc2out = [np.empty((8, 512, TPC), np.float32) for _ in range(NCORES)]
    for j in range(NCORES):
        for i in range(NCORES):
            cc2out[j][i] = r3[i]["cc2in"].reshape(8, 512, TPC)[j]

    nc5 = _build_p5()
    im5 = []
    for c in range(NCORES):
        d = {"cc2out": cc2out[c],
             "z_scr": r1[c]["z_scr"].reshape(B, 16, 128, TPC),
             "xn_scr": r1[c]["xn_scr"].reshape(B, 16, 128, TPC),
             "x_in": np.ascontiguousarray(x[:, c * TPC:(c + 1) * TPC, :]),
             "WO2": WO2, "WU2": WU2, "WG2": WG2, "WD2": WD2}
        d.update(vecs)
        d.update({k: cdict[k] for k in ("c_ident", "c_h128", "c_miscc")})
        im5.append(d)
    r5 = run_bass_kernel_spmd(nc5, im5, core_ids=list(range(NCORES))).results

    y = np.empty((B, T, D), np.float32)
    for c in range(NCORES):
        y[:, c * TPC:(c + 1) * TPC, :] = r5[c]["y_out"].reshape(B, TPC, D)
    return y


def make_wdprep(nc, block, ex, WD2ap, wd2s_ap):
    wdb = sbuf(nc, ex, "wdpb", [128, 8192])
    wtmp = [sbuf(nc, ex, f"wdpt{i}", [128, 128]) for i in range(4)]
    sWL = Sem(nc, ex, "wdpl")
    sWS = Sem(nc, ex, "wdps")
    GW = ES(nc, ex, "wdpg")

    def blk(i):
        return wdb[:, 128 * i:128 * (i + 1)]

    def stages(gp, hhs):
        for hh in hhs:
            los = [i for i in range(64) if not (i & hh)]
            for bat in range(0, 32, 4):
                quad = los[bat:bat + 4]
                for x, i in enumerate(quad):
                    GW.do(gp.tensor_tensor(out=wtmp[x][:, :], in0=blk(i),
                                           in1=blk(i | hh), op=ALU.subtract))
                for i in quad:
                    GW.do(gp.tensor_tensor(out=blk(i), in0=blk(i),
                                           in1=blk(i | hh), op=ALU.add))
                tw = GW.tok
                gp.wait_ge(GW.h, tw)
                for x, i in enumerate(quad):
                    GW.do(gp.tensor_copy(blk(i | hh), wtmp[x][:, :]))
                tw = GW.tok
                gp.wait_ge(GW.h, tw)

    def cb(block, idx):
        if idx >= 64:
            return
        j, ph = idx // 4, idx % 4

        @block.gpsimd
        def _(gp, j=j, ph=ph):
            if ph == 0:
                gp.wait_ge(GW.h, GW.tok)
                gp.wait_ge(sWS.h, sWS.n)
                sWL.inc(gp.dma_start(out=wdb[:, :], in_=WD2ap[j, :, :]), 16)
            elif ph == 1:
                gp.wait_ge(sWL.h, sWL.n)
                stages(gp, (1, 2))
            elif ph == 2:
                gp.wait_ge(GW.h, GW.tok)
                stages(gp, (4, 8))
            else:
                gp.wait_ge(GW.h, GW.tok)
                stages(gp, (16, 32))
                tw = GW.tok
                gp.wait_ge(GW.h, tw)
                sWS.inc(gp.dma_start(out=wd2s_ap[j, :, :], in_=wdb[:, :]), 16)

    cb.sWS = sWS
    return cb

def _build_fused():
    nc = bass.Bass("TRN2", target_bir_lowering=False)
    x_in = nc.dram_tensor("x_in", [B, TPC, D], F32, kind="ExternalInput").ap()
    wT = nc.dram_tensor("wT", [D, IPD], F32, kind="ExternalInput").ap()
    alog4 = nc.dram_tensor("alog4", [4, 1], F32, kind="ExternalInput").ap()
    dtb4 = nc.dram_tensor("dtb4", [4, 1], F32, kind="ExternalInput").ap()
    cwt = nc.dram_tensor("cwt", [128, 24], F32, kind="ExternalInput").ap()
    WO2 = nc.dram_tensor("WO2", [16, 128, 2048], F32, kind="ExternalInput").ap()
    WU2 = nc.dram_tensor("WU2", [64, 128, 2048], F32, kind="ExternalInput").ap()
    WG2 = nc.dram_tensor("WG2", [64, 128, 2048], F32, kind="ExternalInput").ap()
    WD2 = nc.dram_tensor("WD2", [16, 128, 8192], F32, kind="ExternalInput").ap()
    vec = {}
    for n, shp in [("su_t", [128, 16]), ("sudp_t", [128, 64]), ("wsu_t", [128, 64]),
                   ("wsg_t", [128, 64]), ("wsd_t", [128, 16]), ("nw", [128, 1])]:
        vec[n] = nc.dram_tensor(n, shp, F32, kind="ExternalInput").ap()
    y_out = nc.dram_tensor("y_out", [B, TPC, D], F32, kind="ExternalOutput").ap()
    cc1in_t = nc.dram_tensor("cc1in", [8, SH_CH, TPC], F32)
    cc1out_t = nc.dram_tensor("cc1out", [8, SH_CH, TPC], F32)
    cc2in_t = nc.dram_tensor("cc2in", [8, 512, TPC], F32)
    cc2out_t = nc.dram_tensor("cc2out", [8, 512, TPC], F32)
    wd2s_t = nc.dram_tensor("wd2s", [16, 128, 8192], F32)
    z_scr = nc.dram_tensor("z_scr", [B, 16, 128, TPC], F32).ap()
    xn_scr = nc.dram_tensor("xn_scr", [B, 16, 128, TPC], F32).ap()

    with ExitStack() as topex:
        block = topex.enter_context(nc.Block())
        g = load_consts(nc, block, topex, [n for n, _ in CONST_SHAPES])
        cc1d = Sem(nc, topex, "cc1d")
        cc2d = Sem(nc, topex, "cc2d")

        with ExitStack() as ex1:
            p1 = emit_phase1(nc, block, ex1, g, cc1in_t.ap(), z_scr, xn_scr, x_in, wT)

            @block.gpsimd
            def _(gp):
                for s4 in range(4):
                    gp.wait_ge(p1["sST"][s4].h, p1["sST"][s4].n)
                gp.wait_ge(p1["sXN"].h, p1["sXN"].n)
                with allow_ici_all_to_all():
                    gp.collective_compute(
                        "AllToAll", mybir.AluOpType.bypass,
                        replica_groups=[list(range(8))],
                        ins=[cc1in_t.ap().opt()], outs=[cc1out_t.ap().opt()],
                    ).then_inc(cc1d.h, 1)
            cc1d.n = 1

        with ExitStack() as exW:
            wdprep = make_wdprep(nc, block, exW, WD2, wd2s_t.ap())
            with ExitStack() as ex3:
                p3 = emit_phase3(nc, block, ex3, g, cc1out_t.ap(), cc2in_t.ap(),
                                 alog4, dtb4, cwt, entry=(cc1d.h, 1), wdprep=wdprep)

                @block.gpsimd
                def _(gp):
                    for h in range(8):
                        gp.wait_ge(p3["sO"][h].h, p3["sO"][h].n)
                    gp.wait_ge(wdprep.sWS.h, wdprep.sWS.n)
                    with allow_ici_all_to_all():
                        gp.collective_compute(
                            "AllToAll", mybir.AluOpType.bypass,
                            replica_groups=[list(range(8))],
                            ins=[cc2in_t.ap().opt()], outs=[cc2out_t.ap().opt()],
                        ).then_inc(cc2d.h, 1)
                cc2d.n = 1

        with ExitStack() as ex5:
            p5 = emit_phase5(nc, block, ex5, g, cc2out_t.ap(), z_scr, xn_scr,
                             x_in, y_out, WO2, WU2, WG2, wd2s_t.ap(), vec,
                             entry=(cc2d.h, 1))
    return nc


def kernel(**inputs):
    consts = host_constants()
    x = np.ascontiguousarray(inputs["L0_attn_x"], dtype=np.float32)
    wT = np.ascontiguousarray(inputs["L0_attn_in_proj_w"].T.astype(np.float32))
    convw = np.asarray(inputs["L0_attn_conv_w"], np.float32)
    alog = np.asarray(inputs["L0_attn_A_log"], np.float32)
    dtb = np.asarray(inputs["L0_attn_dt_bias"], np.float32)
    nw = np.asarray(inputs["L0_attn_norm_weight"], np.float32)
    WO2 = tile_lhsT(np.ascontiguousarray(inputs["L0_attn_out_proj_w"], dtype=np.float32), 16)
    WU2 = tile_lhsT(np.ascontiguousarray(inputs["L0_ffn_W_up"], dtype=np.float32), 64)
    WG2 = tile_lhsT(np.ascontiguousarray(inputs["L0_ffn_W_gate"], dtype=np.float32), 64)
    WD2 = tile_lhsT(np.ascontiguousarray(inputs["L0_ffn_W_down"], dtype=np.float32), 16)
    su = np.asarray(inputs["L0_ffn_SU_ug"], np.float32)
    sdp = np.asarray(inputs["L0_ffn_SU_dp"], np.float32)
    wsug = np.asarray(inputs["L0_ffn_Wscale_ug"], np.float32)
    wsdp = np.asarray(inputs["L0_ffn_Wscale_dp"], np.float32)
    vecs = dict(su_t=np.ascontiguousarray(su.reshape(16, 128).T),
                sudp_t=np.ascontiguousarray(sdp.reshape(64, 128).T),
                wsu_t=np.ascontiguousarray(wsug[:INTER].reshape(64, 128).T),
                wsg_t=np.ascontiguousarray(wsug[INTER:].reshape(64, 128).T),
                wsd_t=np.ascontiguousarray(wsdp.reshape(16, 128).T),
                nw=np.ascontiguousarray(nw.reshape(128, 1)))
    cdict = {("c_" + n): consts[n] for n, _ in CONST_SHAPES}

    nc = _build_fused()
    im = []
    for c in range(NCORES):
        u = c % 4
        cwt = np.zeros((128, 24), np.float32)
        rows = np.concatenate([convw[128 * u:128 * (u + 1)],
                               convw[512 + 128 * u:512 + 128 * (u + 1)],
                               convw[1024 + 512 * u:1024 + 512 * (u + 1)]], axis=0)
        for t in range(6):
            cwt[:, 4 * t:4 * t + 4] = rows[128 * t:128 * (t + 1)]
        d = {"x_in": np.ascontiguousarray(x[:, c * TPC:(c + 1) * TPC, :]),
             "wT": wT,
             "alog4": np.ascontiguousarray(alog[4 * u:4 * u + 4].reshape(4, 1)),
             "dtb4": np.ascontiguousarray(dtb[4 * u:4 * u + 4].reshape(4, 1)),
             "cwt": cwt,
             "WO2": WO2, "WU2": WU2, "WG2": WG2, "WD2": WD2}
        d.update(vecs)
        d.update(cdict)
        im.append(d)
    res = run_bass_kernel_spmd(nc, im, core_ids=list(range(NCORES))).results
    y = np.empty((B, T, D), np.float32)
    for c in range(NCORES):
        y[:, c * TPC:(c + 1) * TPC, :] = res[c]["y_out"].reshape(B, TPC, D)
    return y
